# version 08
# baseline (speedup 1.0000x reference)
import sys
sys.path.insert(0, '/opt/trn_rl_repo')
import numpy as np
import concourse.bass as bass
import concourse.mybir as mybir

F32 = mybir.dt.float32
BF16 = mybir.dt.bfloat16
Alu = mybir.AluOpType
AF = mybir.ActivationFunctionType
AX = mybir.AxisListType

D = 1024
NB = 8
S_CORE = 512
NH = 16
HD = 64
DH = 4096
EPS = 1e-5
NEG = -1e9

SBUF_EL32 = 46000


class Sem:
    __slots__ = ('handle', 'count', 'name')
    def __init__(self, handle, name):
        self.handle = handle
        self.count = 0
        self.name = name


class SB:

    BASE = 16512
    LIMIT = 229340

    def __init__(self, nc):
        self.nc = nc
        self.top = self.BASE
        self.marks = []
        self.n = 0

    def _alloc(self, name, shape, dtype, esz):
        fs = 1
        for d in shape[1:]:
            fs *= d
        off = (self.top + 31) // 32 * 32
        self.top = off + fs * esz
        assert self.top <= self.LIMIT, f"SBUF overflow {self.top} at {name}"
        self.n += 1
        return self.nc.alloc_sbuf_tensor_at(f"{name}_{self.n}", shape, F32 if esz == 4 else BF16,
                                            offset=off)

    def t4(self, name, shape):
        return self._alloc(name, shape, F32, 4)

    def t2(self, name, shape):
        return self._alloc(name, shape, BF16, 2)

    def mark(self):
        self.marks.append(self.top)

    def release(self):
        self.top = self.marks.pop()


def apx(h, p0=0, pc=None, off=0, dims=None):
    row = 1
    for d in h.shape[1:]:
        row *= d
    pc = h.shape[0] if pc is None else pc
    if dims is None:
        dims = [[1, row - off]]
    return bass.AP(h, p0 * row + off, [[row, pc]] + dims)


def chx(h, chunk, p0=0, pc=None, off=0, n=None):
    w = h.shape[2]
    n = (w - off) if n is None else n
    return apx(h, p0, pc, chunk * w + off, [[1, n]])


class K:
    def __init__(self, nc):
        self.nc = nc
        self.ops = []
        self.sems = {}

    def sem(self, name):
        if name not in self.sems:
            h = self.nc.semaphore(name).__enter__()
            self.sems[name] = Sem(h, name)
        return self.sems[name]

    def on(self, eng, fn):
        self.ops.append((eng, fn))

    def sig(self, eng, semname, n=1):
        s = self.sem(semname)
        s.count += n
        if eng in ('vector', 'scalar', 'tensor', 'gpsimd'):
            self.on(eng, lambda e: e.drain())
        self.on(eng, lambda e, h=s.handle, m=n: e.sem_inc(h, m))
        return s.count

    def wait(self, eng, semname, value=None):
        s = self.sem(semname)
        v = s.count if value is None else value
        if v > 0:
            self.on(eng, lambda e, h=s.handle, v=v: e.wait_ge(h, v))
        return v

    def dma(self, eng, semname, dst, src, **kw):
        s = self.sem(semname)
        s.count += 16
        self.on(eng, lambda e, d=dst, sc=src, h=s.handle, kw=kw:
                e.dma_start(d, sc, **kw).then_inc(h, 16))
        return s.count

    def vec(self, fn):
        self.on('vector', lambda e: e.drain())
        self.on('vector', fn)

    def act(self, fn):
        self.on('scalar', lambda e: e.drain())
        self.on('scalar', fn)

    def gp(self, fn):
        self.on('gpsimd', lambda e: e.drain())
        self.on('gpsimd', fn)

    def te(self, fn):
        self.on('tensor', fn)

    def emit(self):
        ops = self.ops
        self.ops = []
        with self.nc.Block() as block:
            def mk(engname):
                def body(eng):
                    for en, fn in ops:
                        if en == engname:
                            fn(eng)
                return body
            block.sync(mk('sync'))
            block.scalar(mk('scalar'))
            block.vector(mk('vector'))
            block.tensor(mk('tensor'))
            block.gpsimd(mk('gpsimd'))

    def barrier(self):
        self.emit()
        self.nc.all_engine_barrier()


class Banks:

    def __init__(self, nc, k):
        self.k = k
        self.t = [nc.alloc_psum_tensor(f'pb{i}', [128, 512], F32) for i in range(8)]
        self.rel = [(0, 0)] * 8

    def acquire(self, i):
        dv, ac = self.rel[i]
        self.k.wait('tensor', 'dv', dv)
        self.k.wait('tensor', 'ac', ac)
        return self.t[i]

    def release(self, i, eng='vector'):
        dv, ac = self.rel[i]
        if eng == 'vector':
            dv = self.k.sig('vector', 'dv')
        else:
            ac = self.k.sig('scalar', 'ac')
        self.rel[i] = (dv, ac)



def ln_tiles(k, src_h, dst_h, tiles, stats, sq, F=D):
    n = len(tiles)
    for j, (so, _) in enumerate(tiles):
        xt = apx(src_h, 0, 128, so, [[1, F]])
        k.vec(lambda e, xt=xt, j=j: e.reduce_sum(
            apx(stats, 0, 128, 3 * j, [[1, 1]]), xt, AX.X))
        k.vec(lambda e, xt=xt, j=j: e.scalar_tensor_tensor(
            apx(sq), xt, 1.0, xt, Alu.mult, Alu.mult,
            accum_out=apx(stats, 0, 128, 3 * j + 1, [[1, 1]])))
    for j in range(n):
        m = apx(stats, 0, 128, 3 * j, [[1, 1]])
        s2 = apx(stats, 0, 128, 3 * j + 1, [[1, 1]])
        v = apx(stats, 0, 128, 3 * j + 2, [[1, 1]])
        k.vec(lambda e, m=m: e.tensor_scalar(m, m, 1.0 / F, 0.0, Alu.mult, Alu.add))
        k.vec(lambda e, s2=s2: e.tensor_scalar(s2, s2, 1.0 / F, 0.0, Alu.mult, Alu.add))
        k.vec(lambda e, v=v, m=m: e.tensor_tensor(v, m, m, Alu.mult))
        k.vec(lambda e, v=v, s2=s2: e.tensor_tensor(v, s2, v, Alu.subtract))
    dv = k.sig('vector', 'dv')
    k.wait('scalar', 'dv', dv)
    for j in range(n):
        v = apx(stats, 0, 128, 3 * j + 2, [[1, 1]])
        k.act(lambda e, v=v: e.activation(v, v, AF.Sqrt, bias=EPS, scale=1.0))
    ac = k.sig('scalar', 'ac')
    k.wait('vector', 'ac', ac)
    for j in range(n):
        v = apx(stats, 0, 128, 3 * j + 2, [[1, 1]])
        k.vec(lambda e, v=v: e.reciprocal(v, v))
    for j, (so, do) in enumerate(tiles):
        xt = apx(src_h, 0, 128, so, [[1, F]])
        dt_ = apx(dst_h, 0, 128, do, [[1, F]])
        m = apx(stats, 0, 128, 3 * j, [[1, 1]])
        r = apx(stats, 0, 128, 3 * j + 2, [[1, 1]])
        k.vec(lambda e, dt_=dt_, xt=xt, m=m, r=r: e.tensor_scalar(
            dt_, xt, m, r, Alu.subtract, Alu.mult))


def transpose_to_fm_split(k, banks, bank_ids, src_h, src_off, hi_h, lo_h, dst_c0, dst_j,
                          iden_ap, nchunk=8):
    W = hi_h.shape[2]
    per = 4
    for r, b in enumerate(bank_ids):
        pb = banks.acquire(b)
        nq = min(per, nchunk - r * per)
        for q in range(nq):
            c = r * per + q
            k.te(lambda e, pb=pb, q=q, c=c, f=(q == 0), l=(q == nq - 1): e.matmul(
                apx(pb, 0, 128, q * 128, [[1, 128]]),
                apx(src_h, 0, 128, src_off + c * 128, [[1, 128]]),
                iden_ap, start=f, stop=l, is_transpose=True))
        pe = k.sig('tensor', 'pe')
        k.wait('vector', 'pe', pe)
        c0 = r * per
        ncs = min(per, nchunk - c0)
        hdst = apx(hi_h, 0, 128, (dst_c0 + c0) * W + dst_j * 128, [[W, ncs], [1, 128]])
        ldst = apx(lo_h, 0, 128, (dst_c0 + c0) * W + dst_j * 128, [[W, ncs], [1, 128]])
        psrc = apx(pb, 0, 128, 0, [[1, ncs * 128]])
        k.vec(lambda e, hdst=hdst, psrc=psrc: e.tensor_copy(hdst, psrc))
        k.vec(lambda e, ldst=ldst, psrc=psrc, hdst=hdst: e.tensor_tensor(
            ldst, psrc, hdst, Alu.subtract))
        banks.release(b, 'vector')


def mm3(k, pb_ap, wh, wl, xh, xl, first, last=False):
    k.te(lambda e: e.matmul(pb_ap, wh, xh, start=first, stop=False))
    k.te(lambda e: e.matmul(pb_ap, wh, xl, start=False, stop=False))
    k.te(lambda e: e.matmul(pb_ap, wl, xh, start=False, stop=last))


def mm2(k, pb_ap, w, xh, xl, first, last=False):
    k.te(lambda e: e.matmul(pb_ap, w, xh, start=first, stop=False))
    k.te(lambda e: e.matmul(pb_ap, w, xl, start=False, stop=last))


def dram_ap(h, off, dims):
    return bass.AP(h, off, dims)


class Ring:

    _uid = 0

    def __init__(self, k, buf_h, nslots=2):
        self.k = k
        self.h = buf_h
        self.n = nslots
        self.rel_pe = [0] * nslots
        Ring._uid += 1
        self.sems = [f'rg{Ring._uid}_{i}' for i in range(nslots)]
        self.ws = [0] * nslots

    def load(self, slot, src_ap, off=0, n=None):
        self.k.wait('sync', 'pe', self.rel_pe[slot])
        self.ws[slot] = self.k.dma('sync', self.sems[slot],
                                   chx(self.h, slot, off=off, n=n), src_ap)

    def load2(self, slot, dst_ap, src_ap):
        self.ws[slot] = self.k.dma('sync', self.sems[slot], dst_ap, src_ap)

    def use(self, slot):
        self.k.wait('tensor', self.sems[slot], self.ws[slot])

    def done(self, slot):
        self.rel_pe[slot] = self.k.sig('tensor', 'pe')


class Builder:

    def __init__(self, unit=None, n_tiles=8):
        nc = bass.Bass('TRN2', target_bir_lowering=False)
        self.nc = nc
        self.k = K(nc)
        self.sb = SB(nc)
        self.banks = Banks(nc, self.k)
        self.unit = unit
        self.n_tiles = n_tiles
        self.d = {}
        self.s = {}

    def dram(self, name, shape, kind='ExternalInput'):
        if name in self.d:
            return self.d[name]
        h = self.nc.dram_tensor(name, shape, F32, kind=kind)
        self.d[name] = h
        return h

    def dram2(self, name, shape, kind='Internal'):
        h = self.nc.dram_tensor(name, shape, BF16, kind=kind)
        self.d[name] = h
        return h


def p2_declare(B):
    nc, k, sb = B.nc, B.k, B.sb
    d = B.dram
    d('wqt', [D, D]); d('wkt', [D, D]); d('wvt', [D, D])
    d('pev', [128, NH * 8 * 8])
    d('pew', [128, NH * 11])
    d('w1r', [32, 128, 8, 128]); d('w2r', [32, 128, 8, 128])
    d('h1r', [32, 128, 32, 128]); d('h2r', [32, 128, 32, 128])
    d('w3r', [8, 128, 32, 128])
    d('s1f', [128, 32]); d('s2f', [128, 32]); d('hs1f', [128, 32]); d('hs2f', [128, 32])
    d('s3f', [128, 8])
    d('iden', [128, 128]); d('maskl1', [128, 128]); d('oobm', [128, 512])
    d('sel4', [128, 4]); d('delta4', [128, 128])
    import os as _os
    d('y0', [4096, D], kind=('ExternalInput' if B.unit == 'p2' else
                             ('ExternalOutput' if _os.environ.get('K_DBG_Y0') else 'Internal')))
    if B.unit == 'p2':
        d('qf_d', [128, 8 * 512], kind='ExternalOutput')
        d('kf_d', [128, 8 * 512], kind='ExternalOutput')
        d('vp_d', [128, 4 * 1040], kind='ExternalOutput')
        d('qaug_d', [128, 4 * 512], kind='ExternalOutput')
        d('epem_d', [128, 4 * 512], kind='ExternalOutput')
        d('attn_d', [128, 4 * 1024], kind='ExternalOutput')
        d('peob_d', [128, 64], kind='ExternalOutput')
    d('yout', [4096, D], kind='ExternalOutput')
    B.dram2('wqh', [D, D]); B.dram2('wql', [D, D])
    B.dram2('wkh', [D, D]); B.dram2('wkl', [D, D])
    B.dram2('wvh', [D, D]); B.dram2('wvl', [D, D])
    B.dram2('w1b', [32, 128, 8, 128]); B.dram2('w2b', [32, 128, 8, 128])
    B.dram2('h1b', [32, 128, 32, 128]); B.dram2('h2b', [32, 128, 32, 128])
    B.dram2('w3b', [8, 128, 32, 128])
    s = B.s
    for nm, shape in [('IDEN', [128, 128]), ('MASKL1', [128, 128]), ('OOBM', [128, 512]),
                      ('SEL4', [128, 4]), ('DELTA4', [128, 128]), ('PEV', [128, NH * 64]),
                      ('PEW', [128, NH * 11]), ('S1', [128, 32]), ('S2', [128, 32]),
                      ('HS1', [128, 32]), ('HS2', [128, 32]), ('S3', [128, 8])]:
        s[nm] = sb.t4(nm, shape)


def p2_load_consts(B):
    k, s, d = B.k, B.s, B.d
    pairs = [('IDEN', 'iden'), ('MASKL1', 'maskl1'), ('OOBM', 'oobm'), ('SEL4', 'sel4'),
             ('DELTA4', 'delta4'), ('PEV', 'pev'), ('PEW', 'pew'), ('S1', 's1f'),
             ('S2', 's2f'), ('HS1', 'hs1f'), ('HS2', 'hs2f'), ('S3', 's3f')]
    for nm, dn in pairs:
        k.dma('scalar', 'xl', apx(s[nm]), d[dn].ap())


def p2_prep_qkv(B):
    nc, k, sb, d = B.nc, B.k, B.sb, B.d
    for nm in ('pi0', 'pi1', 'po0', 'po1'):
        k.wait('gpsimd', nm)
    sb.mark()
    stg4 = sb.t4('pstg4', [128, 2, 1024])
    stgh = sb.t2('psth', [128, 2, 1024])
    stgl = sb.t2('pstl', [128, 2, 1024])
    rel = [0, 0]
    wrel = [0, 0]
    pin = [0, 0]
    for wi, (src, dsth, dstl) in enumerate([('wqt', 'wqh', 'wql'),
                                            ('wkt', 'wkh', 'wkl'),
                                            ('wvt', 'wvh', 'wvl')]):
        for kc in range(8):
            sl = kc % 2
            k.wait('gpsimd', 'dv', rel[sl])
            pin[sl] = k.dma('gpsimd', f'pi{sl}', chx(stg4, sl),
                            dram_ap(d[src], kc * 128 * D, [[1, 128 * D]]))
            k.wait('vector', f'pi{sl}', pin[sl])
            k.wait('vector', f'po{sl}', wrel[sl])
            k.vec(lambda e, sl=sl: e.tensor_copy(chx(stgh, sl), chx(stg4, sl)))
            k.vec(lambda e, sl=sl: e.tensor_tensor(chx(stgl, sl), chx(stg4, sl),
                                                   chx(stgh, sl), Alu.subtract))
            rel[sl] = k.sig('vector', 'dv')
            k.wait('gpsimd', 'dv', rel[sl])
            k.dma('gpsimd', f'po{sl}', dram_ap(d[dsth], kc * 128 * D, [[1, 128 * D]]),
                  chx(stgh, sl))
            wrel[sl] = k.dma('gpsimd', f'po{sl}',
                             dram_ap(d[dstl], kc * 128 * D, [[1, 128 * D]]),
                             chx(stgl, sl))
    for nm in ('po0', 'po1'):
        k.wait('gpsimd', nm)
    k.sig('gpsimd', 'gp')
    sb.release()


def p2_prep_tern(B):
    nc, k, sb, d = B.nc, B.k, B.sb, B.d
    sb.mark()
    pad = 224992 - ((sb.top + 31) // 32 * 32)
    assert pad >= 0
    if pad:
        sb.t4('tpad', [128, pad // 4])
    stgb = sb.t2('pstb', [128, 2, 1024])
    brel = [0, 0]
    bin_ = [0, 0]
    for src, dst in [('w1r', 'w1b'), ('w2r', 'w2b'),
                     ('h1r', 'h1b'), ('h2r', 'h2b'), ('w3r', 'w3b')]:
        tot = 1
        for x in d[src].shape:
            tot *= x
        ns = tot // (128 * 1024)
        for i in range(ns):
            sl = i % 2
            k.wait('gpsimd', f'bo{sl}', brel[sl])
            bin_[sl] = k.dma('gpsimd', f'bi{sl}', chx(stgb, sl),
                             dram_ap(d[src], i * 128 * 1024, [[1, 128 * 1024]]))
            k.wait('gpsimd', f'bi{sl}', bin_[sl])
            brel[sl] = k.dma('gpsimd', f'bo{sl}',
                             dram_ap(d[dst], i * 128 * 1024, [[1, 128 * 1024]]),
                             chx(stgb, sl))
    for nm in ('bo0', 'bo1'):
        k.wait('gpsimd', nm)
    k.sig('gpsimd', 'gp')
    sb.release()


def p2_alloc(B):
    sb, s = B.sb, B.s
    s['Y0'] = sb.t4('Y0', [128, 4, 1024])
    s['SQ'] = sb.t4('SQ', [128, 1024])
    s['STATS'] = sb.t4('STATS', [128, 12])
    s['XNH'] = sb.t2('XNH', [128, 8, 512])
    s['XNL'] = sb.t2('XNL', [128, 8, 512])
    s['QKVW'] = sb.t2('QKVW', [128, 2, 2048])
    s['W12'] = sb.t2('W12', [128, 2, 2048])
    s['HW'] = sb.t2('HW', [128, 2, 4096])
    s['W3W'] = sb.t2('W3W', [128, 2, 4096])
    sb.mark()
    s['QF'] = sb.t4('QF', [128, 8, 512])
    s['KF'] = sb.t4('KF', [128, 8, 512])
    s['VP'] = sb.t4('VP', [128, 4, 1040])
    s['QAUG'] = sb.t4('QAUG', [128, 4, 512])
    s['EPEM'] = sb.t4('EPEM', [128, 4, 512])
    s['PP'] = sb.t4('PP', [128, 2, 512])
    s['SATT'] = sb.t4('SATT', [128, 2, 512])
    s['DENB'] = sb.t4('DENB', [128, 4])
    s['PEOB'] = sb.t4('PEOB', [128, 4, 16])
    sb.mark()
    s['XN4'] = sb.t4('XN4', [128, 4, 1024])
    sb.release()
    s['ATTN'] = sb.t4('ATTN', [128, 4, 1024])
    sb.release()
    s['HH'] = sb.t2('HH', [128, 32, 256])
    s['HL'] = sb.t2('HL', [128, 32, 256])
    s['GH'] = sb.t2('GH', [128, 32, 256])
    s['GL'] = sb.t2('GL', [128, 32, 256])
    s['W3F'] = sb.t4('W3F', [128, 8, 256])
    s['HF'] = sb.t4('HF', [128, 256])
    s['GF'] = sb.t4('GF', [128, 256])
    pass


def p2_init(B):
    k, s = B.k, B.s
    if 'CONST' not in s:
        s['CONST'] = B.sb.t4('CONST', [128, 4])
        k.vec(lambda e: e.memset(apx(s['CONST'], 0, 128, 0, [[1, 1]]), EPS))
        B.nc.const_aps.register(F32, EPS, apx(s['CONST'], 0, 128, 0, [[1, 1]]))
    k.vec(lambda e: e.memset(apx(s['QAUG']), 0.0))
    k.vec(lambda e: e.memset(apx(s['EPEM']), 0.0))
    k.vec(lambda e: e.memset(apx(s['VP'], 0, 128, 64, [[1040, 4], [65, 16], [1, 1]]), 1.0))
    for pb in B.banks.t:
        k.vec(lambda e, pb=pb: e.memset(apx(pb), 0.0))
    dv = k.sig('vector', 'dv')
    for i in range(8):
        B.banks.rel[i] = (dv, 0)


def p2_tile(B, t, rings):
    k, s, dd, banks = B.k, B.s, B.d, B.banks
    ringq, ring12, ringh, ring3 = rings
    IDENAP = apx(s['IDEN'])
    k.wait('scalar', 'go')
    src = dram_ap(dd['y0'], t * 512 * D, [[D, 128], [128 * D, 4], [1, D]])
    yl = k.dma('scalar', 'yl', apx(s['Y0']), src)
    k.wait('vector', 'yl', yl)
    ln_tiles(k, s['Y0'], s['XN4'], [(j * 1024, j * 1024) for j in range(4)],
             s['STATS'], s['SQ'])
    dv = k.sig('vector', 'dv')
    k.wait('tensor', 'dv', dv)
    for j in range(4):
        transpose_to_fm_split(k, banks, [(2 * j) % 8, (2 * j + 1) % 8], s['XN4'],
                              j * 1024, s['XNH'], s['XNL'], 0, j, IDENAP)
    k.wait('tensor', 'dv')
    for proj, (wh, wl, dsth) in enumerate([('wqh', 'wql', 'QF'), ('wkh', 'wkl', 'KF')]):
        for kc in range(8):
            sl = kc % 2
            ringq.load(sl, dram_ap(dd[wh], kc * 128 * D, [[1, 128 * D]]), off=0, n=1024)
            ringq.load2(sl, chx(s['QKVW'], sl, off=1024, n=1024),
                        dram_ap(dd[wl], kc * 128 * D, [[1, 128 * D]]))
            ringq.use(sl)
            for mc in range(8):
                pb = banks.acquire(mc) if kc == 0 else banks.t[mc]
                mm3(k, apx(pb),
                    chx(s['QKVW'], sl, off=mc * 128, n=128),
                    chx(s['QKVW'], sl, off=1024 + mc * 128, n=128),
                    chx(s['XNH'], kc), chx(s['XNL'], kc),
                    first=(kc == 0), last=(kc == 7))
            ringq.done(sl)
        pe = k.sig('tensor', 'pe')
        k.wait('vector', 'pe', pe)
        for mc in range(8):
            k.vec(lambda e, mc=mc, dsth=dsth: e.tensor_copy(
                chx(s[dsth], mc), apx(banks.t[mc])))
            banks.release(mc, 'vector')
    for kc in range(8):
        sl = kc % 2
        ringq.load(sl, dram_ap(dd['wvh'], kc * 128 * D, [[1, 128 * D]]), off=0, n=1024)
        ringq.load2(sl, chx(s['QKVW'], sl, off=1024, n=1024),
                    dram_ap(dd['wvl'], kc * 128 * D, [[1, 128 * D]]))
        ringq.use(sl)
        for tc in range(4):
            for half in range(2):
                b = tc * 2 + half
                pb = banks.acquire(b) if kc == 0 else banks.t[b]
                rh = chx(s['QKVW'], sl, off=half * 512, n=512)
                rl = chx(s['QKVW'], sl, off=1024 + half * 512, n=512)
                xh = chx(s['XNH'], kc, off=tc * 128, n=128)
                xxl = chx(s['XNL'], kc, off=tc * 128, n=128)
                k.te(lambda e, pb=pb, xh=xh, rh=rh, f=(kc == 0): e.matmul(
                    apx(pb), xh, rh, start=f, stop=False))
                k.te(lambda e, pb=pb, xxl=xxl, rh=rh: e.matmul(
                    apx(pb), xxl, rh, start=False, stop=False))
                k.te(lambda e, pb=pb, xh=xh, rl=rl, l=(kc == 7): e.matmul(
                    apx(pb), xh, rl, start=False, stop=l))
        ringq.done(sl)
    pe = k.sig('tensor', 'pe')
    k.wait('vector', 'pe', pe)
    k.vec(lambda e: e.memset(apx(s['VP'], 0, 128, 64, [[1040, 4], [65, 16], [1, 1]]), 1.0))
    k.vec(lambda e: e.memset(apx(s['EPEM']), 0.0))
    for tc in range(4):
        for half in range(2):
            b = tc * 2 + half
            dst = apx(s['VP'], 0, 128, tc * 1040 + half * 8 * 65, [[65, 8], [1, 64]])
            k.vec(lambda e, b=b, dst=dst: e.tensor_copy(
                dst, apx(banks.t[b], 0, 128, 0, [[64, 8], [1, 64]])))
            banks.release(b, 'vector')
    k.wait('tensor', 'dv')
    for hg in range(4):
        pb = banks.acquire(hg)
        for hs in range(4):
            h = hg * 4 + hs
            for b8 in range(8):
                out = apx(pb, 32 * hs, 8, b8 * 64, [[1, 64]])
                lhs = apx(s['PEV'], (h % 2) * 64, 64, h * 64 + b8 * 8, [[1, 8]])
                rhs = apx(s['QF'], (h % 2) * 64, 64, (h // 2) * 512 + b8, [[8, 64]])
                tp = ((h % 2) * 64, 32 * hs)
                k.te(lambda e, out=out, lhs=lhs, rhs=rhs, f=(b8 == 0), tp=tp,
                     l=(b8 == 7): e.matmul(
                    out, lhs, rhs, start=f, stop=l, tile_position=tp))
        pe = k.sig('tensor', 'pe')
        k.wait('vector', 'pe', pe)
        k.vec(lambda e, hg=hg, pb=pb: e.tensor_copy(chx(s['QAUG'], hg), apx(pb)))
        banks.release(hg, 'vector')
    for hg in range(4):
        pb = banks.acquire(4 + hg)
        for hs in range(4):
            h = hg * 4 + hs
            out = apx(pb, 32 * hs, 11, 0, [[1, 512]])
            lhs = apx(s['PEW'], (h % 2) * 64, 64, h * 11, [[1, 11]])
            rhs = chx(s['QF'], h // 2, (h % 2) * 64, 64)
            tp = ((h % 2) * 64, 32 * hs)
            k.te(lambda e, out=out, lhs=lhs, rhs=rhs, tp=tp: e.matmul(
                out, lhs, rhs, start=True, stop=True, tile_position=tp))
        pe = k.sig('tensor', 'pe')
        k.wait('scalar', 'pe', pe)
        for hs in range(4):
            k.act(lambda e, hg=hg, hs=hs, pb=pb: e.activation(
                apx(s['EPEM'], 32 * hs, 11, hg * 512, [[1, 512]]),
                apx(pb, 32 * hs, 11, 0, [[1, 512]]),
                AF.Exp, bias=0.0, scale=1.0 / 32.0))
        banks.release(4 + hg, 'scalar')
    ac = k.sig('scalar', 'ac')
    k.wait('vector', 'ac', ac)
    for hg in range(4):
        k.vec(lambda e, hg=hg: e.tensor_tensor(
            chx(s['EPEM'], hg), chx(s['EPEM'], hg), apx(s['OOBM']), Alu.mult))
    dv_epem = k.sig('vector', 'dv')
    k.wait('tensor', 'dv', dv_epem)
    pb = banks.acquire(0)
    for grp in range(4):
        for hg in range(4):
            out = apx(pb, 0, 128, grp * 16 + hg * 4, [[1, 4]])
            lhs = apx(s['EPEM'], 0, 128, hg * 512 + grp * 128, [[1, 128]])
            k.te(lambda e, out=out, lhs=lhs, f=(grp == 0 and hg == 0),
                 l=(grp == 3 and hg == 3): e.matmul(
                out, lhs, apx(s['SEL4']), start=f, stop=l))
    pe = k.sig('tensor', 'pe')
    k.wait('vector', 'pe', pe)
    k.vec(lambda e, pb=pb: e.tensor_copy(apx(s['PEOB']), apx(pb, 0, 128, 0, [[1, 64]])))
    banks.release(0, 'vector')
    pp_rel = [0, 0]
    satt_rel = [0, 0]
    pend2 = [None]

    def emit_pv2(grp, hg, slot, acv):
        vb = banks.acquire(4 + hg)
        k.wait('tensor', 'ac', acv)
        for hs in range(4):
            h = hg * 4 + hs
            out = apx(vb, 0, 128, hs * 65, [[1, 65]])
            lhs = apx(s['PP'], 0, 128, slot * 512 + hs * 128, [[1, 128]])
            rhs = apx(s['VP'], 0, 128, grp * 1040 + h * 65, [[1, 65]])
            k.te(lambda e, out=out, lhs=lhs, rhs=rhs, f=(hs == 0),
                 l=(hs == 3): e.matmul(out, lhs, rhs, start=f, stop=l))
        pp_rel[slot] = k.sig('tensor', 'pe')
        pe = pp_rel[slot]
        k.wait('vector', 'pe', pe)
        k.vec(lambda e, vb=vb, grp=grp, hg=hg: e.tensor_tensor(
            apx(s['DENB']), apx(vb, 0, 128, 64, [[65, 4]]),
            apx(s['PEOB'], 0, 128, grp * 16 + hg * 4, [[1, 4]]), Alu.add))
        k.vec(lambda e: e.reciprocal(apx(s['DENB']), apx(s['DENB'])))
        dst = apx(s['ATTN'], 0, 128, grp * 1024 + hg * 256, [[1, 256]])
        k.vec(lambda e, vb=vb, dst=dst: e.tensor_tensor(
            dst, apx(vb, 0, 128, 0, [[65, 4], [1, 64]]),
            apx(s['DENB'], 0, 128, 0, [[1, 4], [0, 64]]), Alu.mult))
        banks.release(4 + hg, 'vector')

    for grp in range(4):
        for hg in range(4):
            slot = (grp * 4 + hg) % 2
            gb = banks.acquire(hg)
            for hs in range(4):
                h = hg * 4 + hs
                out = apx(gb, 0, 128, hs * 128, [[1, 128]])
                lhs = apx(s['KF'], (h % 2) * 64, 64, (h // 2) * 512 + grp * 128, [[1, 128]])
                rhs = apx(s['QF'], (h % 2) * 64, 64, (h // 2) * 512 + grp * 128, [[1, 128]])
                k.te(lambda e, out=out, lhs=lhs, rhs=rhs, f=(hs == 0): e.matmul(
                    out, lhs, rhs, start=f, stop=False))
                lhs2 = apx(s['DELTA4'], 32 * hs, 8)
                rhs2 = apx(s['QAUG'], 32 * hs, 8, hg * 512 + grp * 16, [[1, 16], [64, 8]])
                k.te(lambda e, out=out, lhs2=lhs2, rhs2=rhs2, tp=(32 * hs, 0),
                     l=(hs == 3): e.matmul(
                    out, lhs2, rhs2, start=False, stop=l, tile_position=tp))
            pe = k.sig('tensor', 'pe')
            if pend2[0] is not None:
                emit_pv2(*pend2[0])
            k.wait('vector', 'pe', pe)
            k.wait('vector', 'ac', satt_rel[slot])
            maskrep = apx(s['MASKL1'], 0, 128, 0, [[0, 4], [1, 128]])
            k.vec(lambda e, gb=gb, slot=slot, maskrep=maskrep: e.scalar_tensor_tensor(
                chx(s['SATT'], slot), apx(gb), 1.0 / 32.0, maskrep, Alu.mult, Alu.add))
            banks.release(hg, 'vector')
            dv = k.sig('vector', 'dv')
            k.wait('scalar', 'dv', dv)
            k.wait('scalar', 'pe', pp_rel[slot])
            k.act(lambda e, slot=slot: e.activation(
                chx(s['PP'], slot), chx(s['SATT'], slot), AF.Exp, bias=0.0, scale=1.0))
            satt_rel[slot] = k.sig('scalar', 'ac')
            pend2[0] = (grp, hg, slot, satt_rel[slot])
    emit_pv2(*pend2[0])
    if B.unit == 'p2' and t == 0:
        dv = k.sig('vector', 'dv')
        k.wait('gpsimd', 'dv', dv)
        for nm, hh in [('qf_d', s['QF']), ('kf_d', s['KF']), ('vp_d', s['VP']),
                       ('qaug_d', s['QAUG']), ('epem_d', s['EPEM']),
                       ('attn_d', s['ATTN']), ('peob_d', s['PEOB'])]:
            k.dma('gpsimd', 'go', dd[nm].ap(), apx(hh))
        k.wait('vector', 'go')
        k.wait('tensor', 'go')
        k.wait('scalar', 'go')
    for j in range(4):
        k.vec(lambda e, j=j: e.tensor_add(chx(s['Y0'], j), chx(s['Y0'], j),
                                          chx(s['ATTN'], j)))
    dv = k.sig('vector', 'dv')
    k.wait('tensor', 'dv', dv)
    uprev = [0]
    for sub in range(2):
        off = sub * 256
        for mc in range(32):
            sl = mc % 2
            ring12.load(sl, dram_ap(dd['w1b'], mc * 128 * 1024, [[1, 128 * 1024]]),
                        off=0, n=1024)
            ring12.load2(sl, chx(s['W12'], sl, off=1024, n=1024),
                         dram_ap(dd['w2b'], mc * 128 * 1024, [[1, 128 * 1024]]))
            ring12.use(sl)
            b1 = (mc % 2) * 2
            b2 = b1 + 1
            p1 = banks.acquire(b1)
            p2_ = banks.acquire(b2)
            for kc in range(8):
                w1 = chx(s['W12'], sl, off=kc * 128, n=128)
                w2 = chx(s['W12'], sl, off=1024 + kc * 128, n=128)
                xh = chx(s['XNH'], kc, off=off, n=256)
                xxl = chx(s['XNL'], kc, off=off, n=256)
                mm2(k, apx(p1, 0, 128, 0, [[1, 256]]), w1, xh, xxl,
                    first=(kc == 0), last=(kc == 7))
                mm2(k, apx(p2_, 0, 128, 0, [[1, 256]]), w2, xh, xxl,
                    first=(kc == 0), last=(kc == 7))
            ring12.done(sl)
            pe = k.sig('tensor', 'pe')
            k.wait('scalar', 'pe', pe)
            k.wait('scalar', 'dv', uprev[0])
            k.act(lambda e, p1=p1, mc=mc: e.activation(
                apx(s['HF']), apx(p1, 0, 128, 0, [[1, 256]]), AF.Silu,
                bias=0.0, scale=apx(s['S1'], 0, 128, mc, [[1, 1]])))
            banks.release(b1, 'scalar')
            ac = k.sig('scalar', 'ac')
            k.wait('vector', 'ac', ac)
            k.vec(lambda e, p2_=p2_, mc=mc: e.scalar_tensor_tensor(
                apx(s['GF']), apx(p2_, 0, 128, 0, [[1, 256]]),
                apx(s['S2'], 0, 128, mc, [[1, 1]]), apx(s['HF']), Alu.mult, Alu.mult))
            banks.release(b2, 'vector')
            k.vec(lambda e, mc=mc: e.tensor_copy(chx(s['HH'], mc), apx(s['GF'])))
            k.vec(lambda e, mc=mc: e.tensor_tensor(chx(s['HL'], mc), apx(s['GF']),
                                                   chx(s['HH'], mc), Alu.subtract))
            uprev[0] = k.sig('vector', 'dv')
        k.wait('tensor', 'dv')
        for stage, (wsrc, hin_h, hin_l, gout_h, gout_l, scl) in enumerate([
                ('h1b', 'HH', 'HL', 'GH', 'GL', 'HS1'),
                ('h2b', 'GH', 'GL', 'HH', 'HL', 'HS2')]):
            for mc in range(32):
                sl = mc % 2
                ringh.load(sl, dram_ap(dd[wsrc], mc * 128 * 4096, [[1, 128 * 4096]]))
                ringh.use(sl)
                b = 4 + (mc % 4)
                pb = banks.acquire(b)
                for kc in range(32):
                    w = chx(s['HW'], sl, off=kc * 128, n=128)
                    mm2(k, apx(pb, 0, 128, 0, [[1, 256]]), w,
                        chx(s[hin_h], kc), chx(s[hin_l], kc),
                        first=(kc == 0), last=(kc == 31))
                ringh.done(sl)
                pe = k.sig('tensor', 'pe')
                k.wait('scalar', 'pe', pe)
                k.wait('scalar', 'dv', uprev[0])
                k.act(lambda e, pb=pb, mc=mc, scl=scl: e.activation(
                    apx(s['GF']), apx(pb, 0, 128, 0, [[1, 256]]), AF.Gelu,
                    bias=0.0, scale=apx(s[scl], 0, 128, mc, [[1, 1]])))
                banks.release(b, 'scalar')
                ac = k.sig('scalar', 'ac')
                k.wait('vector', 'ac', ac)
                k.vec(lambda e, mc=mc, gout_h=gout_h: e.tensor_copy(
                    chx(s[gout_h], mc), apx(s['GF'])))
                k.vec(lambda e, mc=mc, gout_h=gout_h, gout_l=gout_l: e.tensor_tensor(
                    chx(s[gout_l], mc), apx(s['GF']), chx(s[gout_h], mc), Alu.subtract))
                uprev[0] = k.sig('vector', 'dv')
            k.wait('tensor', 'dv')
        for mc in range(8):
            sl = mc % 2
            ring3.load(sl, dram_ap(dd['w3b'], mc * 128 * 4096, [[1, 128 * 4096]]))
            ring3.use(sl)
            b = mc % 4
            pb = banks.acquire(b)
            for kc in range(32):
                w = chx(s['W3W'], sl, off=kc * 128, n=128)
                mm2(k, apx(pb, 0, 128, 0, [[1, 256]]), w,
                    chx(s['HH'], kc), chx(s['HL'], kc),
                    first=(kc == 0), last=(kc == 31))
            ring3.done(sl)
            pe = k.sig('tensor', 'pe')
            k.wait('vector', 'pe', pe)
            k.vec(lambda e, pb=pb, mc=mc: e.tensor_scalar(
                chx(s['W3F'], mc), apx(pb, 0, 128, 0, [[1, 256]]),
                apx(s['S3'], 0, 128, mc, [[1, 1]]), 0.0, Alu.mult, Alu.add))
            banks.release(b, 'vector')
        dv = k.sig('vector', 'dv')
        k.wait('tensor', 'dv', dv)
        for jj in range(2):
            j = sub * 2 + jj
            for half in range(2):
                b = 4 + jj * 2 + half
                pb = banks.acquire(b)
                for q in range(4):
                    c = half * 4 + q
                    src = apx(s['W3F'], 0, 128, c * 256 + jj * 128, [[1, 128]])
                    k.te(lambda e, pb=pb, q=q, src=src: e.matmul(
                        apx(pb, 0, 128, q * 128, [[1, 128]]), src, IDENAP,
                        start=(q == 0), stop=(q == 3), is_transpose=True))
            pe = k.sig('tensor', 'pe')
            k.wait('vector', 'pe', pe)
            for half in range(2):
                b = 4 + jj * 2 + half
                dst = apx(s['Y0'], 0, 128, j * 1024 + half * 512, [[1, 512]])
                k.vec(lambda e, dst=dst, b=b: e.tensor_add(dst, dst, apx(banks.t[b])))
                banks.release(b, 'vector')
    dv = k.sig('vector', 'dv')
    k.wait('gpsimd', 'dv', dv)
    dstd = dram_ap(dd['yout'], t * 512 * D, [[D, 128], [128 * D, 4], [1, D]])
    k.dma('gpsimd', 'go', dstd, apx(s['Y0']))


def p2_finish(B):
    B.k.wait('gpsimd', 'go')


def p2_run(B):
    k, s = B.k, B.s
    rings = (Ring(k, s['QKVW']), Ring(k, s['W12']), Ring(k, s['HW']), Ring(k, s['W3W']))
    for t in range(B.n_tiles):
        p2_tile(B, t, rings)
    p2_finish(B)



def p0_declare(B):
    nc, d = B.nc, B.dram
    if B.unit == 'p0':
        d('y0', [4096, D], kind='ExternalOutput')
    d('iden', [128, 128])
    d('xown', [NB, 640, D])
    d('xfull', [32768, D])
    d('qkvw', [D, 3 * D]); d('qkvbf', [128, 24])
    d('rqw', [D, 256]); d('rqbf', [128, 2])
    d('rkvw', [D, 512]); d('rkvbf', [128, 4])
    d('routw', [256, D])
    d('outw', [D, D])
    d('w0t', [D, D]); d('alpha1', [1, 1])
    d('glrep', [128, D]); d('grrep', [128, D])
    d('mkab', [128, 512]); d('mkab0', [128, 512])
    d('mklat', [128, 512])
    d('routbr', [1, D]); d('outbr', [1, D]); d('fbiasr', [1, D]); d('vbr', [1, D])
    B.dram2('qw_h', [D, D]); B.dram2('qw_l', [D, D])
    B.dram2('kw_h', [D, D]); B.dram2('kw_l', [D, D])
    B.dram2('vw_h', [D, D]); B.dram2('vw_l', [D, D])
    B.dram2('ow_h', [D, D]); B.dram2('ow_l', [D, D])
    B.dram2('rq_h', [D, 256]); B.dram2('rq_l', [D, 256])
    d('pool_scr', [512, D], kind='Internal')


def p0_alloc_consts(B):
    sb, s = B.sb, B.s
    for nm, shape in [('QKVB', [128, 24]), ('RQB', [128, 2]),
                      ('RKVB', [128, 4]), ('VBT', [128, D]),
                      ('RBT', [128, D]), ('CBT', [128, D]),
                      ('GLREP', [128, D]), ('GRREP', [128, D]),
                      ('MKAB', [128, 512]), ('MKAB0', [128, 512]),
                      ('MKLAT', [128, 512]),
                      ('ALF', [128, 2])]:
        s[nm] = sb.t4(nm, shape)


def p0_load_consts(B):
    k, s, d = B.k, B.s, B.d
    for nm, dn in [('QKVB', 'qkvbf'), ('RQB', 'rqbf'), ('RKVB', 'rkvbf'),
                   ('GLREP', 'glrep'), ('GRREP', 'grrep'),
                   ('MKAB', 'mkab'), ('MKAB0', 'mkab0'), ('MKLAT', 'mklat')]:
        k.dma('scalar', 'xl', apx(s[nm]), d[dn].ap())
    k.dma('scalar', 'xl', apx(s['VBT']), dram_ap(d['vbr'], 0, [[0, 128], [1, D]]))
    k.dma('scalar', 'xl', apx(s['RBT']), dram_ap(d['routbr'], 0, [[0, 128], [1, D]]))
    k.dma('scalar', 'xl', apx(s['CBT']), dram_ap(d['outbr'], 0, [[0, 128], [1, D]]))


def p0_bias_init(B):
    k, s, d = B.k, B.s, B.d
    k.wait('scalar', 'pe')
    fb = k.dma('scalar', 'ye', apx(s['SQ0']), dram_ap(d['fbiasr'], 0, [[0, 128], [1, D]]))
    k.wait('vector', 'ye', fb)
    k.wait('vector', 'xl')
    k.vec(lambda e: e.scalar_tensor_tensor(
        apx(s['CBT']), apx(s['SQ0']), apx(s['ALF'], 0, 128, 1, [[1, 1]]),
        apx(s['CBT']), Alu.mult, Alu.add))
    k.sig('vector', 'dv')


def p0_prep_weights(B):
    nc, k, sb, s, d = B.nc, B.k, B.sb, B.s, B.d
    sb.mark()
    stg4 = sb.t4('qstg4', [128, 2, 1024])
    stgh = sb.t2('qsth', [128, 2, 1024])
    stgl = sb.t2('qstl', [128, 2, 1024])
    asums = sb.t4('asums', [128, 10])
    onesr = sb.t4('onesr', [1, 128])
    rel = [0, 0]
    wrel = [0, 0]
    pin = [0, 0]
    jobs = []
    for kc in range(8):
        for third, (dh, dl) in enumerate([('qw_h', 'qw_l'), ('kw_h', 'kw_l'),
                                          ('vw_h', 'vw_l')]):
            jobs.append((d['qkvw'], kc * 128 * 3072 + third * 1024, 3072,
                         d[dh], kc * 128 * D, d[dl]))
    for kc in range(8):
        jobs.append((d['outw'], kc * 128 * D, D, d['ow_h'], kc * 128 * D, d['ow_l']))
    for kc in range(8):
        jobs.append((d['rqw'], kc * 128 * 256, 256, d['rq_h'], kc * 128 * 256,
                     d['rq_l']))
    for ji, (srch, soff, srow, dsth, doff, dstl) in enumerate(jobs):
        sl = ji % 2
        n = 1024 if srow != 256 else 256
        k.wait('gpsimd', 'dv', rel[sl])
        pin[sl] = k.dma('gpsimd', f'pi{sl}', chx(stg4, sl, n=n),
                        dram_ap(srch, soff, [[srow, 128], [1, n]]))
        k.wait('vector', f'pi{sl}', pin[sl])
        k.wait('vector', f'po{sl}', wrel[sl])
        k.vec(lambda e, sl=sl, n=n: e.tensor_copy(chx(stgh, sl, n=n), chx(stg4, sl, n=n)))
        k.vec(lambda e, sl=sl, n=n: e.tensor_tensor(chx(stgl, sl, n=n), chx(stg4, sl, n=n),
                                                    chx(stgh, sl, n=n), Alu.subtract))
        rel[sl] = k.sig('vector', 'dv')
        k.wait('gpsimd', 'dv', rel[sl])
        k.dma('gpsimd', f'po{sl}', dram_ap(dsth, doff, [[1, 128 * n]]), chx(stgh, sl, n=n))
        wrel[sl] = k.dma('gpsimd', f'po{sl}', dram_ap(dstl, doff, [[1, 128 * n]]),
                         chx(stgl, sl, n=n))
    for kc in range(8):
        sl = kc % 2
        k.wait('gpsimd', 'dv', rel[sl])
        pin[sl] = k.dma('gpsimd', f'pi{sl}', chx(stg4, sl),
                        dram_ap(d['w0t'], kc * 128 * D, [[1, 128 * D]]))
        k.wait('vector', f'pi{sl}', pin[sl])
        k.vec(lambda e, sl=sl, kc=kc: e.tensor_reduce(
            apx(asums, 0, 128, kc, [[1, 1]]), chx(stg4, sl), AX.X, Alu.add,
            apply_absolute_value=True))
        rel[sl] = k.sig('vector', 'dv')
    k.vec(lambda e: e.memset(apx(s['ONE1']), 1.0))
    dv = k.sig('vector', 'dv')
    k.wait('tensor', 'dv', dv)
    pb = B.banks.acquire(0)
    k.te(lambda e: e.matmul(apx(pb, 0, 1, 0, [[1, 8]]), apx(s['ONE1']),
                            apx(asums, 0, 128, 0, [[1, 8]]), start=True, stop=True))
    pe = k.sig('tensor', 'pe')
    k.wait('vector', 'pe', pe)
    k.vec(lambda e: e.reduce_sum(apx(asums, 0, 1, 8, [[1, 1]]),
                                 apx(pb, 0, 1, 0, [[1, 8]]), AX.X))
    B.banks.release(0, 'vector')
    k.vec(lambda e: e.tensor_scalar(apx(asums, 0, 1, 8, [[1, 1]]),
                                    apx(asums, 0, 1, 8, [[1, 1]]),
                                    0.7 / (1024.0 * 1024.0), 0.0, Alu.mult, Alu.add))
    dv = k.sig('vector', 'dv')
    k.wait('tensor', 'dv', dv)
    k.vec(lambda e: e.memset(apx(onesr), 1.0))
    dvx = k.sig('vector', 'dv')
    k.wait('tensor', 'dv', dvx)
    onesrow = apx(onesr)
    pb = B.banks.acquire(1)
    k.te(lambda e, onesrow=onesrow: e.matmul(apx(pb, 0, 128, 0, [[1, 1]]),
                            onesrow,
                            apx(asums, 0, 1, 8, [[1, 1]]), start=True, stop=False))
    k.on('sync', lambda e: None)
    alf_stage = apx(s['ALF'], 0, 1, 1, [[1, 1]])
    xl = k.dma('scalar', 'xl', alf_stage, d['alpha1'].ap())
    k.wait('tensor', 'xl', xl)
    k.te(lambda e, onesrow=onesrow: e.matmul(apx(pb, 0, 128, 1, [[1, 1]]),
                            onesrow,
                            alf_stage, start=False, stop=True))
    pe = k.sig('tensor', 'pe')
    k.wait('vector', 'pe', pe)
    k.vec(lambda e: e.tensor_copy(apx(s['ALF'], 0, 128, 0, [[1, 2]]),
                                  apx(pb, 0, 128, 0, [[1, 2]])))
    B.banks.release(1, 'vector')
    for kc in range(8):
        sl = kc % 2
        k.wait('gpsimd', 'dv', rel[sl])
        pin[sl] = k.dma('gpsimd', f'pi{sl}', chx(stg4, sl),
                        dram_ap(d['w0t'], kc * 128 * D, [[1, 128 * D]]))
        k.wait('vector', f'pi{sl}', pin[sl])
        k.wait('vector', f'po{sl}', wrel[sl])
        k.vec(lambda e, sl=sl: e.tensor_scalar(
            chx(stgh, sl), chx(stg4, sl), apx(s['ALF'], 0, 128, 0, [[1, 1]]), None,
            Alu.is_gt))
        k.vec(lambda e, sl=sl: e.tensor_scalar(
            chx(stgl, sl), chx(stg4, sl), -1.0, apx(s['ALF'], 0, 128, 0, [[1, 1]]),
            Alu.mult, Alu.is_gt))
        k.vec(lambda e, sl=sl, kc=kc: e.tensor_tensor(
            chx(s['TERN'], kc), chx(stgh, sl), chx(stgl, sl), Alu.subtract))
        rel[sl] = k.sig('vector', 'dv')
    for nm in ('po0', 'po1'):
        k.wait('gpsimd', nm)
    k.sig('gpsimd', 'gp')
    sb.release()


def p0_alloc(B):
    sb, s = B.sb, B.s
    s['SQ0'] = sb.t4('SQ0', [128, 1024])
    s['ST0'] = sb.t4('ST0', [128, 16])
    s['TERN'] = sb.t2('TERN', [128, 8, 1024])
    s['XFH'] = sb.t2('XFH', [128, 8, 640])
    s['XFL'] = sb.t2('XFL', [128, 8, 640])
    s['RING0'] = sb.t2('RING0', [128, 2, 2048])
    s['ROUTW'] = sb.t4('ROUTW', [128, 2, 1024])
    s['RKVF'] = sb.t4('RKVF', [128, 4, 512])
    s['RVP'] = sb.t4('RVP', [128, 16, 65])
    s['XEP'] = sb.t4('XEP', [128, 1024])
    s['T1E'] = sb.t4('T1E', [128, 1024])
    s['ONE1'] = sb.t4('ONE1', [128, 1])
    s['PL0'] = sb.t4('PL0', [128, 4, 256])
    sb.mark()
    s['PLAT'] = sb.t4('PLAT', [128, 4, 512])
    sb.release()
    s['ESL'] = sb.t4('ESL', [128, 2, 1024])
    s['LOC'] = sb.t4('LOC', [128, 4, 1024])
    s['LAT'] = sb.t4('LAT', [128, 4, 256])
    s['DEN0'] = sb.t4('DEN0', [128, 4])
    s['GATE'] = sb.t4('GATE', [128, 16, 2])
    sb.mark()
    s['X0'] = sb.t4('X0', [128, 5, 1024])
    sb.release()
    sb.mark()
    s['XT'] = sb.t4('XT', [128, 2, 4096])
    s['RQ4'] = sb.t4('RQ4', [128, 8])
    s['CORR'] = sb.t4('CORR', [2, 2])
    s['PSTG'] = sb.t4('PSTG', [2, 4, 1024])
    sb.release()
    sb.mark()
    s['POOLF'] = sb.t4('POOLF', [128, 8, 512])
    sb.mark()
    s['POOLTM'] = sb.t4('POOLTM', [128, 4, 1024])
    sb.release()
    s['RKVW'] = sb.t4('RKVW', [128, 8, 512])
    sb.release()
    sb.mark()
    s['QF0'] = sb.t4('QF0', [128, 8, 512])
    s['KF0'] = sb.t4('KF0', [128, 8, 640])
    s['VP0'] = sb.t4('VP0', [128, 5, 1040])
    s['RQF'] = sb.t4('RQF', [128, 2, 512])
    sb.release()
    s['LATF'] = sb.t4('LATF', [128, 2, 512])
    s['REM'] = sb.t4('REM', [128, 4, 1024])
    s['MIXFH'] = sb.t2('MIXFH', [128, 8, 512])
    s['MIXFL'] = sb.t2('MIXFL', [128, 8, 512])
    pass


def transpose_to_fm(k, banks, bank_ids, src_h, src_off, dst_h, dst_c0, dst_j,
                    iden_ap, nchunk=8, W=None):
    W = dst_h.shape[2] if W is None else W
    per = 4
    for r, b in enumerate(bank_ids):
        pb = banks.acquire(b)
        nq = min(per, nchunk - r * per)
        for q in range(nq):
            c = r * per + q
            k.te(lambda e, pb=pb, q=q, c=c, f=(q == 0), l=(q == nq - 1): e.matmul(
                apx(pb, 0, 128, q * 128, [[1, 128]]),
                apx(src_h, 0, 128, src_off + c * 128, [[1, 128]]),
                iden_ap, start=f, stop=l, is_transpose=True))
        pe = k.sig('tensor', 'pe')
        k.wait('vector', 'pe', pe)
        c0 = r * per
        ncs = min(per, nchunk - c0)
        dst = apx(dst_h, 0, 128, (dst_c0 + c0) * W + dst_j * 128, [[W, ncs], [1, 128]])
        k.vec(lambda e, dst=dst, pb=pb, ncs=ncs: e.tensor_copy(
            dst, apx(pb, 0, 128, 0, [[1, ncs * 128]])))
        banks.release(b, 'vector')


def p0_pha(B):
    k, s, d, banks = B.k, B.s, B.d, B.banks
    for nm in ('pi0', 'pi1', 'po0', 'po1'):
        k.wait('vector', nm)
    st = s['ST0']
    xa = [0, 0]
    rel = [0, 0]
    for g in range(64):
        sl = g % 2
        k.wait('scalar', 'dv', rel[sl])
        xa[sl] = k.dma('scalar', f'xa{sl}', chx(s['XT'], sl),
                       dram_ap(d['xfull'], g * 512 * D,
                               [[D, 128], [128 * D, 4], [1, D]]))
        k.wait('vector', f'xa{sl}', xa[sl])
        k.vec(lambda e, sl=sl: e.reduce_sum(
            apx(st, 0, 128, 0, [[1, 4]]),
            apx(s['XT'], 0, 128, sl * 4096, [[1024, 4], [1, 1024]]), AX.X))
        for t4 in range(4):
            k.vec(lambda e, sl=sl, t4=t4: e.scalar_tensor_tensor(
                apx(s['SQ0']), apx(s['XT'], 0, 128, sl * 4096 + t4 * 1024, [[1, 1024]]),
                1.0, apx(s['XT'], 0, 128, sl * 4096 + t4 * 1024, [[1, 1024]]),
                Alu.mult, Alu.mult, accum_out=apx(st, 0, 128, 4 + t4, [[1, 1]])))
        k.vec(lambda e: e.tensor_scalar(apx(st, 0, 128, 0, [[1, 4]]),
                                        apx(st, 0, 128, 0, [[1, 4]]), 1.0 / D, 0.0,
                                        Alu.mult, Alu.add))
        k.vec(lambda e: e.tensor_scalar(apx(st, 0, 128, 4, [[1, 4]]),
                                        apx(st, 0, 128, 4, [[1, 4]]), 1.0 / D, 0.0,
                                        Alu.mult, Alu.add))
        k.vec(lambda e: e.tensor_tensor(apx(st, 0, 128, 8, [[1, 4]]),
                                        apx(st, 0, 128, 0, [[1, 4]]),
                                        apx(st, 0, 128, 0, [[1, 4]]), Alu.mult))
        k.vec(lambda e: e.tensor_tensor(apx(st, 0, 128, 8, [[1, 4]]),
                                        apx(st, 0, 128, 4, [[1, 4]]),
                                        apx(st, 0, 128, 8, [[1, 4]]), Alu.subtract))
        dv = k.sig('vector', 'dv')
        k.wait('scalar', 'dv', dv)
        k.act(lambda e: e.activation(apx(st, 0, 128, 8, [[1, 4]]),
                                     apx(st, 0, 128, 8, [[1, 4]]), AF.Sqrt,
                                     bias=EPS, scale=1.0))
        ac = k.sig('scalar', 'ac')
        k.wait('vector', 'ac', ac)
        k.vec(lambda e: e.reciprocal(apx(st, 0, 128, 12, [[1, 4]]),
                                     apx(st, 0, 128, 8, [[1, 4]])))
        k.vec(lambda e: e.memset(apx(s['RQ4']), 0.0))
        k.vec(lambda e: e.tensor_copy(apx(s['RQ4'], 0, 64, 0, [[2, 4]]),
                                      apx(st, 0, 64, 12, [[1, 4]])))
        k.vec(lambda e: e.tensor_copy(apx(s['RQ4'], 64, 64, 1, [[2, 4]]),
                                      apx(st, 64, 64, 12, [[1, 4]])))
        dv = k.sig('vector', 'dv')
        k.wait('tensor', 'dv', dv)
        for t4 in range(4):
            pa = banks.acquire(2 * t4)
            pbb = banks.acquire(2 * t4 + 1)
            lhsR = apx(s['RQ4'], 0, 128, t4 * 2, [[1, 2]])
            k.te(lambda e, pa=pa, lhsR=lhsR, sl=sl, t4=t4: e.matmul(
                apx(pa, 0, 2, 0, [[1, 512]]), lhsR,
                apx(s['XT'], 0, 128, sl * 4096 + t4 * 1024, [[1, 512]]),
                start=True, stop=True))
            k.te(lambda e, pbb=pbb, lhsR=lhsR, sl=sl, t4=t4: e.matmul(
                apx(pbb, 0, 2, 0, [[1, 512]]), lhsR,
                apx(s['XT'], 0, 128, sl * 4096 + t4 * 1024 + 512, [[1, 512]]),
                start=True, stop=True))
        rel[sl] = k.sig('tensor', 'pe')
        k.wait('vector', 'pe', rel[sl])
        k.wait('vector', 'go')
        for t4 in range(4):
            pa, pbb = banks.t[2 * t4], banks.t[2 * t4 + 1]
            k.vec(lambda e, pa=pa, t4=t4: e.reduce_sum(
                apx(s['CORR'], 0, 2, 0, [[1, 1]]), apx(pa, 0, 2, 0, [[1, 512]]), AX.X))
            k.vec(lambda e, pbb=pbb, t4=t4: e.reduce_sum(
                apx(s['CORR'], 0, 2, 1, [[1, 1]]), apx(pbb, 0, 2, 0, [[1, 512]]), AX.X))
            k.vec(lambda e: e.tensor_tensor(
                apx(s['CORR'], 0, 2, 0, [[1, 1]]), apx(s['CORR'], 0, 2, 0, [[1, 1]]),
                apx(s['CORR'], 0, 2, 1, [[1, 1]]), Alu.add))
            k.vec(lambda e: e.tensor_scalar(
                apx(s['CORR'], 0, 2, 0, [[1, 1]]), apx(s['CORR'], 0, 2, 0, [[1, 1]]),
                1.0 / D, 0.0, Alu.mult, Alu.add))
            k.vec(lambda e, pa=pa, t4=t4: e.tensor_scalar(
                apx(s['PSTG'], 0, 2, t4 * 1024, [[1, 512]]), apx(pa, 0, 2, 0, [[1, 512]]),
                apx(s['CORR'], 0, 2, 0, [[1, 1]]), 1.0 / 64.0, Alu.subtract, Alu.mult))
            banks.release(2 * t4, 'vector')
            k.vec(lambda e, pbb=pbb, t4=t4: e.tensor_scalar(
                apx(s['PSTG'], 0, 2, t4 * 1024 + 512, [[1, 512]]),
                apx(pbb, 0, 2, 0, [[1, 512]]),
                apx(s['CORR'], 0, 2, 0, [[1, 1]]), 1.0 / 64.0, Alu.subtract, Alu.mult))
            banks.release(2 * t4 + 1, 'vector')
        rel[sl] = k.sig('vector', 'dv')
        k.wait('gpsimd', 'dv', rel[sl])
        k.dma('gpsimd', 'go',
              dram_ap(d['pool_scr'], g * 8 * D, [[D, 2], [2 * D, 4], [1, D]]),
              apx(s['PSTG']))
    k.wait('gpsimd', 'go')
    k.sig('gpsimd', 'gp')


def p0_rkv(B, core):
    k, s, d, banks = B.k, B.s, B.d, B.banks
    IDENAP = apx(B.s['IDEN'])
    k.wait('scalar', 'gp')
    xl = k.dma('scalar', 'yl', apx(s['POOLTM']), dram_ap(d['pool_scr'], 0, [[1, 512 * D]]))
    k.wait('vector', 'yl', xl)
    k.wait('tensor', 'yl', xl)
    for j in range(4):
        transpose_to_fm(k, banks, [2 * j, 2 * j + 1], s['POOLTM'], j * 1024,
                        s['POOLF'], 0, j, IDENAP)
    dvp = k.sig('vector', 'dv')
    k.wait('scalar', 'dv', dvp)
    wl = k.dma('scalar', 'xl', apx(s['RKVW']),
               dram_ap(d['rkvw'], 0, [[512, 128], [128 * 512, 8], [1, 512]]))
    k.wait('tensor', 'xl', wl)
    k.wait('tensor', 'dv')
    for kc in range(8):
        for m in range(4):
            pb = banks.acquire(m) if kc == 0 else banks.t[m]
            k.te(lambda e, pb=pb, kc=kc, m=m, f=(kc == 0), l=(kc == 7): e.matmul(
                apx(pb), apx(s['RKVW'], 0, 128, kc * 512 + m * 128, [[1, 128]]),
                chx(s['POOLF'], kc), start=f, stop=l))
    pe = k.sig('tensor', 'pe')
    k.wait('vector', 'pe', pe)
    for m in range(4):
        k.vec(lambda e, m=m: e.tensor_scalar(
            chx(s['RKVF'], m), apx(banks.t[m]),
            apx(s['RKVB'], 0, 128, m, [[1, 1]]), 0.0, Alu.add, Alu.add))
        banks.release(m, 'vector')
    dv = k.sig('vector', 'dv')
    k.wait('tensor', 'dv', dv)
    k.vec(lambda e: e.memset(apx(s['RVP'], 0, 128, 64, [[65, 16], [1, 1]]), 1.0))
    dv = k.sig('vector', 'dv')
    k.wait('tensor', 'dv', dv)
    for t8 in range(8):
        cc = 2 + t8 // 4
        i = t8 % 4
        pb = banks.acquire(4 + t8 % 4)
        src = apx(s['RKVF'], 0, 128, cc * 512 + i * 128, [[1, 128]])
        k.te(lambda e, pb=pb, src=src: e.matmul(
            apx(pb, 0, 128, 0, [[1, 128]]), src, IDENAP,
            start=True, stop=True, is_transpose=True))
        pe = k.sig('tensor', 'pe')
        k.wait('vector', 'pe', pe)
        for lhin in range(2):
            lh = (cc - 2) * 2 + lhin
            blk = i * 4 + lh
            dst = apx(s['RVP'], 0, 128, blk * 65, [[1, 64]])
            k.vec(lambda e, pb=pb, dst=dst, lhin=lhin: e.tensor_copy(
                dst, apx(pb, 0, 128, lhin * 64, [[1, 64]])))
        banks.release(4 + t8 % 4, 'vector')


def p0_batch(B, b, ring0):
    k, s, dd, banks = B.k, B.s, B.d, B.banks
    IDENAP = apx(s['IDEN'])
    k.wait('scalar', 'yo')
    yl = k.dma('scalar', 'yl', apx(s['X0']), dram_ap(dd['xown'], b * 640 * D,
                                                     [[D, 128], [128 * D, 5], [1, D]]))
    k.wait('vector', 'yl', yl)
    ln_tiles(k, s['X0'], s['X0'], [(j * 1024, j * 1024) for j in range(5)],
             s['ST0'], s['SQ0'])
    dv = k.sig('vector', 'dv')
    k.wait('tensor', 'dv', dv)
    for j in range(5):
        transpose_to_fm_split(k, banks, [(2 * j) % 8, (2 * j + 1) % 8], s['X0'],
                              j * 1024, s['XFH'], s['XFL'], 0, j, IDENAP)
    k.wait('tensor', 'dv')

    def stream_fm(dsth, scrh, scrl, wrow, nmch, rhs_off, rhs_n, biasT, bias_c0):
        nsp = (rhs_n + 511) // 512
        mg = max(1, 8 // nsp)
        for m0 in range(0, nmch, mg):
            mge = min(mg, nmch - m0)
            for kc in range(8):
                sl = kc % 2
                ring0.load(sl, dram_ap(dd[scrh], kc * 128 * wrow + m0 * 128,
                                       [[wrow, 128], [1, mge * 128]]), off=0, n=mge * 128)
                ring0.load2(sl, chx(s['RING0'], sl, off=1024, n=mge * 128),
                            dram_ap(dd[scrl], kc * 128 * wrow + m0 * 128,
                                    [[wrow, 128], [1, mge * 128]]))
                ring0.use(sl)
                for mi in range(mge):
                    for ns in range(nsp):
                        nn = min(512, rhs_n - ns * 512)
                        bk = mi * nsp + ns
                        pb = banks.acquire(bk) if kc == 0 else banks.t[bk]
                        xh = chx(s['XFH'], kc, off=rhs_off + ns * 512, n=nn)
                        xxl = chx(s['XFL'], kc, off=rhs_off + ns * 512, n=nn)
                        wh = chx(s['RING0'], sl, off=mi * 128, n=128)
                        wl = chx(s['RING0'], sl, off=1024 + mi * 128, n=128)
                        mm3(k, apx(pb, 0, 128, 0, [[1, nn]]), wh, wl, xh, xxl,
                            first=(kc == 0), last=(kc == 7))
                ring0.done(sl)
            pe = k.sig('tensor', 'pe')
            k.wait('vector', 'pe', pe)
            for mi in range(mge):
                for ns in range(nsp):
                    nn = min(512, rhs_n - ns * 512)
                    bk = mi * nsp + ns
                    k.vec(lambda e, mi=mi, ns=ns, nn=nn, bk=bk, m0=m0: e.tensor_scalar(
                        chx(s[dsth], m0 + mi, off=ns * 512, n=nn),
                        apx(banks.t[bk], 0, 128, 0, [[1, nn]]),
                        apx(s[biasT], 0, 128, bias_c0 + m0 + mi, [[1, 1]]), 0.0,
                        Alu.add, Alu.add))
                    banks.release(bk, 'vector')

    stream_fm('QF0', 'qw_h', 'qw_l', D, 8, 128, 512, 'QKVB', 0)
    stream_fm('KF0', 'kw_h', 'kw_l', D, 8, 0, 640, 'QKVB', 8)
    stream_fm('RQF', 'rq_h', 'rq_l', 256, 2, 128, 512, 'RQB', 0)

    k.vec(lambda e: e.memset(apx(s['VP0'], 0, 128, 64, [[1040, 5], [65, 16], [1, 1]]), 1.0))
    for rnd in range(10):
        gc, half = rnd // 2, rnd % 2
        bk = rnd % 2 + 6
        for kc in range(8):
            sl = kc % 2
            ring0.load(sl, dram_ap(dd['vw_h'], kc * 128 * D + half * 512,
                                   [[D, 128], [1, 512]]), off=0, n=512)
            ring0.load2(sl, chx(s['RING0'], sl, off=1024, n=512),
                        dram_ap(dd['vw_l'], kc * 128 * D + half * 512,
                                [[D, 128], [1, 512]]))
            ring0.use(sl)
            pb = banks.acquire(bk) if kc == 0 else banks.t[bk]
            xh = chx(s['XFH'], kc, off=gc * 128, n=128)
            xxl = chx(s['XFL'], kc, off=gc * 128, n=128)
            wh = chx(s['RING0'], sl, n=512)
            wl = chx(s['RING0'], sl, off=1024, n=512)
            k.te(lambda e, pb=pb, xh=xh, wh=wh, f=(kc == 0): e.matmul(
                apx(pb), xh, wh, start=f, stop=False))
            k.te(lambda e, pb=pb, xxl=xxl, wh=wh: e.matmul(
                apx(pb), xxl, wh, start=False, stop=False))
            k.te(lambda e, pb=pb, xh=xh, wl=wl, l=(kc == 7): e.matmul(
                apx(pb), xh, wl, start=False, stop=l))
            ring0.done(sl)
        pb = banks.t[bk]
        pe = k.sig('tensor', 'pe')
        k.wait('vector', 'pe', pe)
        dst = apx(s['VP0'], 0, 128, gc * 1040 + half * 8 * 65, [[65, 8], [1, 64]])
        vbt = apx(s['VBT'], 0, 128, half * 512, [[1, 512]])
        k.vec(lambda e, pb=pb, dst=dst, vbt=vbt: e.tensor_tensor(
            dst, apx(pb, 0, 128, 0, [[1, 512]]), vbt, Alu.add))
        banks.release(bk, 'vector')
    dv = k.sig('vector', 'dv')
    k.wait('tensor', 'dv', dv)

    esl_free = [0, 0]
    pv_sig = [0]

    def emit_pv(qb, hg, dvv):
        vb = banks.acquire(hg)
        k.wait('tensor', 'dv', dvv)
        for hs in range(4):
            h = hg * 4 + hs
            rhsV = apx(s['VP0'], 0, 128, qb * 1040 + h * 65, [[1, 65]])
            rhsV2 = apx(s['VP0'], 0, 128, (qb + 1) * 1040 + h * 65, [[1, 65]])
            lhsP = apx(s['PL0'], 0, 128, hs * 256, [[1, 128]])
            lhsP2 = apx(s['PL0'], 0, 128, hs * 256 + 128, [[1, 128]])
            k.te(lambda e, vb=vb, lhsP=lhsP, rhsV=rhsV, hs=hs, f=(hs == 0): e.matmul(
                apx(vb, 0, 128, hs * 65, [[1, 65]]), lhsP, rhsV,
                start=f, stop=False))
            k.te(lambda e, vb=vb, lhsP2=lhsP2, rhsV2=rhsV2, hs=hs, l=(hs == 3): e.matmul(
                apx(vb, 0, 128, hs * 65, [[1, 65]]), lhsP2, rhsV2,
                start=False, stop=l))
        pv_sig[0] = k.sig('tensor', 'pe')
        k.wait('vector', 'pe', pv_sig[0])
        k.vec(lambda e, vb=vb: e.tensor_scalar(
            apx(s['DEN0']), apx(vb, 0, 128, 64, [[65, 4]]), 1e-12, 0.0,
            Alu.add, Alu.add))
        k.vec(lambda e: e.reciprocal(apx(s['DEN0']), apx(s['DEN0'])))
        dst = apx(s['LOC'], 0, 128, qb * 1024 + hg * 256, [[1, 256]])
        k.vec(lambda e, vb=vb, dst=dst: e.tensor_tensor(
            dst, apx(vb, 0, 128, 0, [[65, 4], [1, 64]]),
            apx(s['DEN0'], 0, 128, 0, [[1, 4], [0, 64]]), Alu.mult))
        banks.release(hg, 'vector')

    pend = None
    for i, (qb, hg) in enumerate([(q, h) for q in range(4) for h in range(4)]):
        sl = i % 2
        gb4 = banks.acquire(4)
        gb5 = banks.acquire(5)
        for hs in range(4):
            h = hg * 4 + hs
            gb = gb4 if hs % 2 == 0 else gb5
            co = (hs // 2) * 256
            base = (h % 2) * 64
            lhsA = chx(s['KF0'], h // 2, base, 64, off=qb * 128, n=128)
            lhsB = chx(s['KF0'], h // 2, base, 64, off=qb * 128 + 128, n=128)
            rhsq = chx(s['QF0'], h // 2, base, 64, off=qb * 128, n=128)
            k.te(lambda e, gb=gb, lhsA=lhsA, rhsq=rhsq, co=co, f=(hs < 2):
                 e.matmul(apx(gb, 0, 128, co, [[1, 128]]), lhsA, rhsq,
                          start=f, stop=False))
            k.te(lambda e, gb=gb, lhsB=lhsB, rhsq=rhsq, co=co, l=(hs >= 2):
                 e.matmul(apx(gb, 0, 128, co + 128, [[1, 128]]), lhsB, rhsq,
                          start=False, stop=l))
        pe_g = k.sig('tensor', 'pe')
        if pend is not None:
            emit_pv(*pend)
        k.wait('scalar', 'pe', pe_g)
        k.wait('scalar', 'dv', esl_free[sl])
        for pi, gb in enumerate((gb4, gb5)):
            k.act(lambda e, gb=gb, sl=sl, pi=pi: e.activation(
                apx(s['ESL'], 0, 128, sl * 1024 + pi * 512, [[1, 512]]),
                apx(gb, 0, 128, 0, [[1, 512]]), AF.Exp, bias=0.0, scale=0.125))
        banks.release(4, 'scalar')
        banks.release(5, 'scalar')
        ac_i = k.sig('scalar', 'ac')
        k.wait('vector', 'ac', ac_i)
        k.wait('vector', 'pe', pv_sig[0])
        mk = 'MKAB0' if qb == 0 else 'MKAB'
        for pi in range(2):
            k.vec(lambda e, sl=sl, pi=pi, mk=mk: e.tensor_tensor(
                apx(s['PL0'], 0, 128, pi * 256, [[512, 2], [1, 256]]),
                apx(s['ESL'], 0, 128, sl * 1024 + pi * 512, [[1, 512]]),
                apx(s[mk]), Alu.mult))
        esl_free[sl] = k.sig('vector', 'dv')
        pend = (qb, hg, esl_free[sl])
    emit_pv(*pend)

    k.wait('scalar', 'dv')
    for lh in range(4):
        base = (lh % 2) * 64
        gb = banks.acquire(4 + lh)
        lhsK = apx(s['RKVF'], base, 64, (lh // 2) * 512 + b * 64, [[1, 64]])
        rhsq = chx(s['RQF'], lh // 2, base, 64)
        k.te(lambda e, gb=gb, lhsK=lhsK, rhsq=rhsq: e.matmul(
            apx(gb, 0, 64, 0, [[1, 512]]), lhsK, rhsq, start=True, stop=True))
        k.te(lambda e, gb=gb, lhsK=lhsK, rhsq=rhsq, tp=(base, 64): e.matmul(
            apx(gb, 64, 64, 0, [[1, 512]]), lhsK, rhsq, start=True, stop=True,
            tile_position=tp))
        pe = k.sig('tensor', 'pe')
        k.wait('scalar', 'pe', pe)
        k.act(lambda e, gb=gb, lh=lh: e.activation(
            chx(s['PLAT'], lh), apx(gb, 0, 128, 0, [[1, 512]]), AF.Exp,
            bias=0.0, scale=0.125))
        banks.release(4 + lh, 'scalar')
    ac = k.sig('scalar', 'ac')
    k.wait('vector', 'ac', ac)
    for lh in range(4):
        k.vec(lambda e, lh=lh: e.tensor_tensor(
            chx(s['PLAT'], lh), chx(s['PLAT'], lh), apx(s['MKLAT']),
            Alu.mult))
    dv = k.sig('vector', 'dv')
    k.wait('tensor', 'dv', dv)
    for qc in range(4):
        vb = banks.acquire(qc % 4)
        for lh in range(4):
            base = (b % 2) * 64
            blk = (b // 2) * 4 + lh
            lhsP = apx(s['PLAT'], base, 64, lh * 512 + qc * 128, [[1, 128]])
            rhsV = apx(s['RVP'], base, 64, blk * 65, [[1, 65]])
            outp = apx(vb, 0, 128, lh * 65, [[1, 65]])
            k.te(lambda e, outp=outp, lhsP=lhsP, rhsV=rhsV, f=(lh == 0), l=(lh == 3):
                 e.matmul(outp, lhsP, rhsV, start=f, stop=l))
        pe = k.sig('tensor', 'pe')
        k.wait('vector', 'pe', pe)
        k.vec(lambda e, vb=vb: e.tensor_scalar(
            apx(s['DEN0']), apx(vb, 0, 128, 64, [[65, 4]]), 1e-12, 0.0,
            Alu.add, Alu.add))
        k.vec(lambda e: e.reciprocal(apx(s['DEN0']), apx(s['DEN0'])))
        dst = chx(s['LAT'], qc)
        k.vec(lambda e, vb=vb, dst=dst: e.tensor_tensor(
            dst, apx(vb, 0, 128, 0, [[65, 4], [1, 64]]),
            apx(s['DEN0'], 0, 128, 0, [[1, 4], [0, 64]]), Alu.mult))
        banks.release(qc % 4, 'vector')
    dv = k.sig('vector', 'dv')
    k.wait('tensor', 'dv', dv)
    for qc in range(4):
        bkt = 4 + qc % 4
        pb = banks.acquire(bkt)
        for cc in range(2):
            src = apx(s['LAT'], 0, 128, qc * 256 + cc * 128, [[1, 128]])
            k.te(lambda e, pb=pb, src=src, cc=cc, f=(cc == 0), l=(cc == 1): e.matmul(
                apx(pb, 0, 128, cc * 128, [[1, 128]]), src, IDENAP,
                start=f, stop=l, is_transpose=True))
        pe = k.sig('tensor', 'pe')
        k.wait('vector', 'pe', pe)
        dst = apx(s['LATF'], 0, 128, qc * 128, [[512, 2], [1, 128]])
        k.vec(lambda e, pb=pb, dst=dst: e.tensor_copy(dst, apx(pb, 0, 128, 0, [[1, 256]])))
        banks.release(bkt, 'vector')
    dv = k.sig('vector', 'dv')
    k.wait('tensor', 'dv', dv)
    for qc in range(4):
        b0 = (qc % 2) * 2
        p1 = banks.acquire(b0)
        p2_ = banks.acquire(b0 + 1)
        for kc in range(2):
            lhs = apx(s['LATF'], 0, 128, kc * 512 + qc * 128, [[1, 128]])
            for ns in range(2):
                pb = (p1, p2_)[ns]
                rhs = chx(s['ROUTW'], kc, off=ns * 512, n=512)
                k.te(lambda e, pb=pb, lhs=lhs, rhs=rhs, f=(kc == 0), l=(kc == 1):
                     e.matmul(apx(pb), lhs, rhs, start=f, stop=l))
        pe = k.sig('tensor', 'pe')
        k.wait('vector', 'pe', pe)
        for ns in range(2):
            pb = (p1, p2_)[ns]
            rbt = apx(s['RBT'], 0, 128, ns * 512, [[1, 512]])
            k.vec(lambda e, pb=pb, qc=qc, ns=ns, rbt=rbt: e.tensor_tensor(
                apx(s['REM'], 0, 128, qc * 1024 + ns * 512, [[1, 512]]), apx(pb),
                rbt, Alu.add))
            banks.release(b0 + ns, 'vector')
    dv = k.sig('vector', 'dv')
    k.wait('scalar', 'dv', dv)
    for qc in range(4):
        k.vec(lambda e, qc=qc: e.tensor_tensor(
            apx(s['SQ0']), chx(s['LOC'], qc), apx(s['GLREP']), Alu.mult))
        k.vec(lambda e, qc=qc: e.reduce_sum(
            apx(s['GATE'], 0, 128, 0, [[2, 16]]),
            apx(s['SQ0'], 0, 128, 0, [[64, 16], [1, 64]]), AX.X))
        k.vec(lambda e, qc=qc: e.tensor_tensor(
            apx(s['SQ0']), chx(s['REM'], qc), apx(s['GRREP']), Alu.mult))
        k.vec(lambda e, qc=qc: e.reduce_sum(
            apx(s['GATE'], 0, 128, 1, [[2, 16]]),
            apx(s['SQ0'], 0, 128, 0, [[64, 16], [1, 64]]), AX.X))
        k.vec(lambda e: e.tensor_tensor(
            apx(s['GATE'], 0, 128, 0, [[2, 16]]), apx(s['GATE'], 0, 128, 0, [[2, 16]]),
            apx(s['GATE'], 0, 128, 1, [[2, 16]]), Alu.add))
        dv = k.sig('vector', 'dv')
        k.wait('scalar', 'dv', dv)
        k.act(lambda e: e.activation(apx(s['GATE'], 0, 128, 0, [[2, 16]]),
                                     apx(s['GATE'], 0, 128, 0, [[2, 16]]),
                                     AF.Sigmoid, bias=0.0, scale=1.0))
        ac = k.sig('scalar', 'ac')
        k.wait('vector', 'ac', ac)
        k.vec(lambda e, qc=qc: e.tensor_tensor(
            chx(s['LOC'], qc), chx(s['LOC'], qc), chx(s['REM'], qc), Alu.subtract))
        k.vec(lambda e, qc=qc: e.tensor_tensor(
            chx(s['LOC'], qc), chx(s['LOC'], qc),
            apx(s['GATE'], 0, 128, 0, [[2, 16], [0, 64]]), Alu.mult))
        k.vec(lambda e, qc=qc: e.tensor_tensor(
            chx(s['LOC'], qc), chx(s['LOC'], qc), chx(s['REM'], qc), Alu.add))
    dv = k.sig('vector', 'dv')
    k.wait('tensor', 'dv', dv)
    for j in range(4):
        transpose_to_fm_split(k, banks, [(2 * j) % 8, (2 * j + 1) % 8], s['LOC'],
                              j * 1024, s['MIXFH'], s['MIXFL'], 0, j, IDENAP)
    k.wait('tensor', 'dv')
    for qc in range(4):
        p1 = banks.acquire(0)
        p2_ = banks.acquire(1)
        for kc in range(8):
            sl = kc % 2
            ring0.load(sl, dram_ap(dd['ow_h'], kc * 128 * D, [[1, 128 * D]]),
                       off=0, n=1024)
            ring0.load2(sl, chx(s['RING0'], sl, off=1024, n=1024),
                        dram_ap(dd['ow_l'], kc * 128 * D, [[1, 128 * D]]))
            ring0.use(sl)
            lh = chx(s['MIXFH'], kc, off=qc * 128, n=128)
            ll = chx(s['MIXFL'], kc, off=qc * 128, n=128)
            for ns in range(2):
                pb = (p1, p2_)[ns]
                wh = chx(s['RING0'], sl, off=ns * 512, n=512)
                wl = chx(s['RING0'], sl, off=1024 + ns * 512, n=512)
                mm3(k, apx(pb), lh, ll, wh, wl, first=(kc == 0), last=(kc == 7))
            ring0.done(sl)
        p3 = banks.acquire(2)
        p4 = banks.acquire(3)
        for kc in range(8):
            lh = chx(s['XFH'], kc, off=128 + qc * 128, n=128)
            ll = chx(s['XFL'], kc, off=128 + qc * 128, n=128)
            for ns in range(2):
                pb = (p3, p4)[ns]
                w = chx(s['TERN'], kc, off=ns * 512, n=512)
                k.te(lambda e, pb=pb, lh=lh, w=w, f=(kc == 0): e.matmul(
                    apx(pb), lh, w, start=f, stop=False))
                k.te(lambda e, pb=pb, ll=ll, w=w, l=(kc == 7): e.matmul(
                    apx(pb), ll, w, start=False, stop=l))
        pe = k.sig('tensor', 'pe')
        k.wait('scalar', 'dv')
        xe = k.dma('scalar', 'ye', apx(s['XEP']),
                   dram_ap(dd['xown'], (b * 640 + 128 + qc * 128) * D, [[1, 128 * D]]))
        k.wait('vector', 'ye', xe)
        k.wait('vector', 'pe', pe)
        k.wait('vector', 'yo')
        for ns in range(2):
            pf = (p3, p4)[ns]
            po = (p1, p2_)[ns]
            sl512 = apx(s['T1E'], 0, 128, ns * 512, [[1, 512]])
            cbt = apx(s['CBT'], 0, 128, ns * 512, [[1, 512]])
            k.vec(lambda e, pf=pf, sl512=sl512, cbt=cbt: e.scalar_tensor_tensor(
                sl512, apx(pf), apx(s['ALF'], 0, 128, 1, [[1, 1]]), cbt,
                Alu.mult, Alu.add))
            k.vec(lambda e, po=po, sl512=sl512: e.tensor_tensor(
                sl512, sl512, apx(po), Alu.add))
            k.vec(lambda e, ns=ns, sl512=sl512: e.tensor_tensor(
                sl512, sl512, apx(s['XEP'], 0, 128, ns * 512, [[1, 512]]), Alu.add))
        for ns in range(4):
            banks.release(ns, 'vector')
        dv = k.sig('vector', 'dv')
        k.wait('scalar', 'dv', dv)
        k.dma('scalar', 'yo', dram_ap(dd['y0'], (qc * 128 * 8 + b) * D,
                                      [[8 * D, 128], [1, D]]), apx(s['T1E']))


def p0_run(B, nb=NB):
    k, s = B.k, B.s
    ring0 = Ring(k, s['RING0'])
    xl = k.dma('scalar', 'xl', apx(s['ROUTW']),
               dram_ap(B.d['routw'], 0, [[D, 128], [128 * D, 2], [1, D]]))
    k.wait('tensor', 'xl', xl)
    for b in range(nb):
        p0_batch(B, b, ring0)
    k.wait('gpsimd', 'yo')

from concourse.bass_utils import run_bass_kernel_spmd

_CACHED = {}


def build_full(skip_p0=False, skip_p2=False):
    import os
    skip_p0 = skip_p0 or bool(os.environ.get('K_SKIP_P0'))
    skip_p2 = skip_p2 or bool(os.environ.get('K_SKIP_P2'))
    if 'nc' in _CACHED:
        return _CACHED['nc']
    B = Builder(unit='full', n_tiles=8)
    k, sb, s = B.k, B.sb, B.s
    p0_declare(B)
    p2_declare(B)
    p0_alloc_consts(B)
    s['CONST'] = sb.t4('CONST', [128, 4])
    k.vec(lambda e: e.memset(apx(s['CONST'], 0, 128, 0, [[1, 1]]), EPS))
    B.nc.const_aps.register(F32, EPS, apx(s['CONST'], 0, 128, 0, [[1, 1]]))
    p0_load_consts(B)
    p2_load_consts(B)
    sb.mark()
    p0_alloc(B)
    k.wait('vector', 'xl')
    k.wait('tensor', 'xl')
    k.wait('scalar', 'xl')
    p0_prep_weights(B)
    p2_prep_qkv(B)
    p0_bias_init(B)
    stage = os.environ.get('K_P0_STAGE', 'all')
    if not skip_p0:
        if stage in ('pha', 'rkv', 'all') or stage.startswith('b'):
            p0_pha(B)
            B.k.barrier()
        if stage in ('rkv', 'all') or stage.startswith('b'):
            p0_rkv(B, 0)
            B.k.barrier()
        p2_prep_tern(B)
        if stage == 'all' or stage.startswith('b'):
            p0_run(B, nb=(NB if stage == 'all' else int(stage[1:])))
            B.k.barrier()
    else:
        p2_prep_tern(B)
        B.k.barrier()
    sb.release()
    sb.mark()
    p2_alloc(B)
    p2_init(B)
    if not skip_p2:
        p2_run(B)
    B.k.barrier()
    sb.release()
    _CACHED['nc'] = B.nc
    _CACHED['B'] = B
    return B.nc


def _consts_per_core(core):
    d = {}
    d['iden'] = np.eye(128, dtype=np.float32)
    r = np.arange(128)[:, None]
    c = np.arange(128)[None, :]
    mA = np.where(r >= c, 0.0, NEG).astype(np.float32)
    mB = np.where(r < c, 0.0, NEG).astype(np.float32)
    mk = np.ascontiguousarray(np.concatenate([mA, mB], axis=1))
    mk0 = (np.ascontiguousarray(
        np.concatenate([np.full((128, 128), NEG, np.float32), mB], axis=1))
        if core == 0 else mk)
    d['mkab'] = np.ascontiguousarray(np.tile((mk == 0).astype(np.float32), (1, 2)))
    d['mkab0'] = np.ascontiguousarray(np.tile((mk0 == 0).astype(np.float32), (1, 2)))
    lat = np.arange(64)[:, None]
    qq = np.arange(512)[None, :]
    mkl = (lat < (8 * core + qq // 64)).astype(np.float32)
    d['mklat'] = np.ascontiguousarray(np.concatenate([mkl, mkl], axis=0))

    m = np.full((128, 128), NEG, np.float32)
    for s1 in range(16):
        for b1 in range(8):
            for b in range(8):
                if abs(b1 - b) <= 5:
                    m[s1 * 8 + b1, s1 * 8 + b] = 0.0
    d['maskl1'] = m
    o = np.zeros((128, 512), np.float32)
    for hs in range(4):
        for w in range(11):
            for b in range(8):
                j = b + w - 5
                if j < 0 or j >= 8:
                    o[32 * hs + w, b::8] = 1.0
    d['oobm'] = o
    sel = np.zeros((128, 4), np.float32)
    for n in range(4):
        sel[32 * n:32 * n + 11, n] = 1.0
    d['sel4'] = sel
    dl = np.zeros((128, 128), np.float32)
    for sl in range(4):
        for s1 in range(16):
            for b1 in range(8):
                dl[32 * sl + b1, s1 * 8 + b1] = 1.0
    d['delta4'] = dl
    return d


def stage_inputs(inputs):
    x = np.ascontiguousarray(np.asarray(inputs['L0_attn_x'], dtype=np.float32))
    shared = {}
    shared['xfull'] = np.ascontiguousarray(x.reshape(32768, D))
    shared['qkvw'] = np.ascontiguousarray(np.asarray(inputs['L0_attn_qkv_w']))
    shared['qkvbf'] = np.ascontiguousarray(
        np.asarray(inputs['L0_attn_qkv_b']).reshape(24, 128).T)
    shared['rqw'] = np.ascontiguousarray(np.asarray(inputs['L0_attn_rq_w']))
    shared['rqbf'] = np.ascontiguousarray(
        np.asarray(inputs['L0_attn_rq_b']).reshape(2, 128).T)
    shared['rkvw'] = np.ascontiguousarray(np.asarray(inputs['L0_attn_rkv_w']))
    shared['rkvbf'] = np.ascontiguousarray(
        np.asarray(inputs['L0_attn_rkv_b']).reshape(4, 128).T)
    shared['routw'] = np.ascontiguousarray(np.asarray(inputs['L0_attn_rout_w']))

    shared['outw'] = np.ascontiguousarray(np.asarray(inputs['L0_attn_out_w']))

    shared['w0t'] = np.ascontiguousarray(np.asarray(inputs['L0_ffn_weight']).T)
    shared['alpha1'] = np.asarray(inputs['L0_ffn_alpha']).reshape(1, 1)
    shared['routbr'] = np.asarray(inputs['L0_attn_rout_b']).reshape(1, D)
    shared['outbr'] = np.asarray(inputs['L0_attn_out_b']).reshape(1, D)
    shared['fbiasr'] = np.asarray(inputs['L0_ffn_bias']).reshape(1, D)
    shared['vbr'] = np.ascontiguousarray(
        np.asarray(inputs['L0_attn_qkv_b'])[2048:].reshape(1, D))
    gl = np.asarray(inputs['L0_attn_gl_w']).ravel()
    gr = np.asarray(inputs['L0_attn_gr_w']).ravel()
    shared['glrep'] = np.ascontiguousarray(
        np.tile(np.tile(gl, 16)[None, :], (128, 1)).astype(np.float32))
    shared['grrep'] = np.ascontiguousarray(
        np.tile(np.tile(gr, 16)[None, :], (128, 1)).astype(np.float32))

    shared['wqt'] = np.ascontiguousarray(np.asarray(inputs['L1_attn_Wq']).T)
    shared['wkt'] = np.ascontiguousarray(np.asarray(inputs['L1_attn_Wk']).T)
    shared['wvt'] = np.ascontiguousarray(np.asarray(inputs['L1_attn_Wv']).T)
    pe = np.asarray(inputs['L1_attn_pe'])
    pev = np.zeros((64, 16, 8, 8), np.float32)
    for b in range(8):
        for j in range(8):
            w = j - b + 5
            if 0 <= w < 11:
                pev[:, :, b, j] = pe[:, :, w].T
    shared['pev'] = np.ascontiguousarray(
        np.concatenate([pev.reshape(64, -1)] * 2, axis=0))
    pew = pe.transpose(1, 0, 2).reshape(64, 16 * 11)
    shared['pew'] = np.ascontiguousarray(np.concatenate([pew] * 2, axis=0))
    shared['w1r'] = np.ascontiguousarray(
        np.asarray(inputs['L1_ffn_w1_w']).T.reshape(8, 128, 32, 128).transpose(2, 1, 0, 3))
    shared['w2r'] = np.ascontiguousarray(
        np.asarray(inputs['L1_ffn_w2_w']).T.reshape(8, 128, 32, 128).transpose(2, 1, 0, 3))
    shared['h1r'] = np.ascontiguousarray(
        np.asarray(inputs['L1_ffn_h1_w']).T.reshape(32, 128, 32, 128).transpose(2, 1, 0, 3))
    shared['h2r'] = np.ascontiguousarray(
        np.asarray(inputs['L1_ffn_h2_w']).T.reshape(32, 128, 32, 128).transpose(2, 1, 0, 3))
    shared['w3r'] = np.ascontiguousarray(
        np.asarray(inputs['L1_ffn_w3_w']).T.reshape(32, 128, 8, 128).transpose(2, 1, 0, 3))
    shared['s1f'] = np.ascontiguousarray(np.asarray(inputs['L1_ffn_w1_s']).reshape(32, 128).T)
    shared['s2f'] = np.ascontiguousarray(np.asarray(inputs['L1_ffn_w2_s']).reshape(32, 128).T)
    shared['hs1f'] = np.ascontiguousarray(np.asarray(inputs['L1_ffn_h1_s']).reshape(32, 128).T)
    shared['hs2f'] = np.ascontiguousarray(np.asarray(inputs['L1_ffn_h2_s']).reshape(32, 128).T)
    shared['s3f'] = np.ascontiguousarray(np.asarray(inputs['L1_ffn_w3_s']).reshape(8, 128).T)
    maps = []
    for core in range(8):
        d = dict(shared)
        d.update(_consts_per_core(core))
        xo = np.zeros((8, 640, D), np.float32)
        s0 = core * 512
        for b in range(NB):
            if s0 >= 128:
                xo[b, :128] = x[b, s0 - 128:s0]
            xo[b, 128:] = x[b, s0:s0 + 512]
        d['xown'] = xo
        maps.append(d)
    return maps


def kernel(**inputs):
    nc = build_full()
    maps = stage_inputs(inputs)
    res = run_bass_kernel_spmd(nc, maps, core_ids=list(range(8)))
    out = np.empty((NB, 4096, D), dtype=np.float32)
    for i in range(8):
        yo = np.asarray(res.results[i]['yout']).reshape(512, 8, D)
        out[:, i * 512:(i + 1) * 512, :] = yo.transpose(1, 0, 2)
    return out


# version 09 (final)
# speedup vs baseline: 1.0670x; 1.0670x over previous
import sys
sys.path.insert(0, '/opt/trn_rl_repo')
import numpy as np
import concourse.bass as bass
import concourse.mybir as mybir

F32 = mybir.dt.float32
BF16 = mybir.dt.bfloat16
Alu = mybir.AluOpType
AF = mybir.ActivationFunctionType
AX = mybir.AxisListType

D = 1024
NB = 8
S_CORE = 512
NH = 16
HD = 64
DH = 4096
EPS = 1e-5
NEG = -1e9

SBUF_EL32 = 46000


class Sem:
    __slots__ = ('handle', 'count', 'name')
    def __init__(self, handle, name):
        self.handle = handle
        self.count = 0
        self.name = name


class SB:

    BASE = 16512
    LIMIT = 229340

    def __init__(self, nc):
        self.nc = nc
        self.top = self.BASE
        self.marks = []
        self.n = 0

    def _alloc(self, name, shape, dtype, esz):
        fs = 1
        for d in shape[1:]:
            fs *= d
        off = (self.top + 31) // 32 * 32
        self.top = off + fs * esz
        assert self.top <= self.LIMIT, f"SBUF overflow {self.top} at {name}"
        self.n += 1
        return self.nc.alloc_sbuf_tensor_at(f"{name}_{self.n}", shape, F32 if esz == 4 else BF16,
                                            offset=off)

    def t4(self, name, shape):
        return self._alloc(name, shape, F32, 4)

    def t2(self, name, shape):
        return self._alloc(name, shape, BF16, 2)

    def mark(self):
        self.marks.append(self.top)

    def release(self):
        self.top = self.marks.pop()


def apx(h, p0=0, pc=None, off=0, dims=None):
    row = 1
    for d in h.shape[1:]:
        row *= d
    pc = h.shape[0] if pc is None else pc
    if dims is None:
        dims = [[1, row - off]]
    return bass.AP(h, p0 * row + off, [[row, pc]] + dims)


def chx(h, chunk, p0=0, pc=None, off=0, n=None):
    w = h.shape[2]
    n = (w - off) if n is None else n
    return apx(h, p0, pc, chunk * w + off, [[1, n]])


class K:
    def __init__(self, nc):
        self.nc = nc
        self.ops = []
        self.sems = {}

    def sem(self, name):
        if name not in self.sems:
            h = self.nc.semaphore(name).__enter__()
            self.sems[name] = Sem(h, name)
        return self.sems[name]

    def on(self, eng, fn):
        self.ops.append((eng, fn))

    def sig(self, eng, semname, n=1):
        s = self.sem(semname)
        s.count += n
        if eng in ('vector', 'scalar', 'tensor', 'gpsimd'):
            self.on(eng, lambda e: e.drain())
        self.on(eng, lambda e, h=s.handle, m=n: e.sem_inc(h, m))
        return s.count

    def wait(self, eng, semname, value=None):
        s = self.sem(semname)
        v = s.count if value is None else value
        if v > 0:
            self.on(eng, lambda e, h=s.handle, v=v: e.wait_ge(h, v))
        return v

    def dma(self, eng, semname, dst, src, **kw):
        s = self.sem(semname)
        s.count += 16
        self.on(eng, lambda e, d=dst, sc=src, h=s.handle, kw=kw:
                e.dma_start(d, sc, **kw).then_inc(h, 16))
        return s.count

    def vec(self, fn):
        self.on('vector', lambda e: e.drain())
        self.on('vector', fn)

    def act(self, fn):
        self.on('scalar', lambda e: e.drain())
        self.on('scalar', fn)

    def gp(self, fn):
        self.on('gpsimd', lambda e: e.drain())
        self.on('gpsimd', fn)

    def te(self, fn):
        self.on('tensor', fn)

    def emit(self):
        ops = self.ops
        self.ops = []
        with self.nc.Block() as block:
            def mk(engname):
                def body(eng):
                    for en, fn in ops:
                        if en == engname:
                            fn(eng)
                return body
            block.sync(mk('sync'))
            block.scalar(mk('scalar'))
            block.vector(mk('vector'))
            block.tensor(mk('tensor'))
            block.gpsimd(mk('gpsimd'))

    def barrier(self):
        self.emit()
        self.nc.all_engine_barrier()


class Banks:

    def __init__(self, nc, k):
        self.k = k
        self.t = [nc.alloc_psum_tensor(f'pb{i}', [128, 512], F32) for i in range(8)]
        self.rel = [(0, 0)] * 8

    def acquire(self, i):
        dv, ac = self.rel[i]
        self.k.wait('tensor', 'dv', dv)
        self.k.wait('tensor', 'ac', ac)
        return self.t[i]

    def release(self, i, eng='vector'):
        dv, ac = self.rel[i]
        if eng == 'vector':
            dv = self.k.sig('vector', 'dv')
        else:
            ac = self.k.sig('scalar', 'ac')
        self.rel[i] = (dv, ac)



def ln_tiles(k, src_h, dst_h, tiles, stats, sq, F=D):
    n = len(tiles)
    for j, (so, _) in enumerate(tiles):
        xt = apx(src_h, 0, 128, so, [[1, F]])
        k.vec(lambda e, xt=xt, j=j: e.reduce_sum(
            apx(stats, 0, 128, 3 * j, [[1, 1]]), xt, AX.X))
        k.vec(lambda e, xt=xt, j=j: e.scalar_tensor_tensor(
            apx(sq), xt, 1.0, xt, Alu.mult, Alu.mult,
            accum_out=apx(stats, 0, 128, 3 * j + 1, [[1, 1]])))
    for j in range(n):
        m = apx(stats, 0, 128, 3 * j, [[1, 1]])
        s2 = apx(stats, 0, 128, 3 * j + 1, [[1, 1]])
        v = apx(stats, 0, 128, 3 * j + 2, [[1, 1]])
        k.vec(lambda e, m=m: e.tensor_scalar(m, m, 1.0 / F, 0.0, Alu.mult, Alu.add))
        k.vec(lambda e, s2=s2: e.tensor_scalar(s2, s2, 1.0 / F, 0.0, Alu.mult, Alu.add))
        k.vec(lambda e, v=v, m=m: e.tensor_tensor(v, m, m, Alu.mult))
        k.vec(lambda e, v=v, s2=s2: e.tensor_tensor(v, s2, v, Alu.subtract))
    dv = k.sig('vector', 'dv')
    k.wait('scalar', 'dv', dv)
    for j in range(n):
        v = apx(stats, 0, 128, 3 * j + 2, [[1, 1]])
        k.act(lambda e, v=v: e.activation(v, v, AF.Sqrt, bias=EPS, scale=1.0))
    ac = k.sig('scalar', 'ac')
    k.wait('vector', 'ac', ac)
    for j in range(n):
        v = apx(stats, 0, 128, 3 * j + 2, [[1, 1]])
        k.vec(lambda e, v=v: e.reciprocal(v, v))
    for j, (so, do) in enumerate(tiles):
        xt = apx(src_h, 0, 128, so, [[1, F]])
        dt_ = apx(dst_h, 0, 128, do, [[1, F]])
        m = apx(stats, 0, 128, 3 * j, [[1, 1]])
        r = apx(stats, 0, 128, 3 * j + 2, [[1, 1]])
        k.vec(lambda e, dt_=dt_, xt=xt, m=m, r=r: e.tensor_scalar(
            dt_, xt, m, r, Alu.subtract, Alu.mult))


def transpose_to_fm_split(k, banks, bank_ids, src_h, src_off, hi_h, lo_h, dst_c0, dst_j,
                          iden_ap, nchunk=8):
    W = hi_h.shape[2]
    per = 4
    for r, b in enumerate(bank_ids):
        pb = banks.acquire(b)
        nq = min(per, nchunk - r * per)
        for q in range(nq):
            c = r * per + q
            k.te(lambda e, pb=pb, q=q, c=c, f=(q == 0), l=(q == nq - 1): e.matmul(
                apx(pb, 0, 128, q * 128, [[1, 128]]),
                apx(src_h, 0, 128, src_off + c * 128, [[1, 128]]),
                iden_ap, start=f, stop=l, is_transpose=True))
        pe = k.sig('tensor', 'pe')
        k.wait('vector', 'pe', pe)
        c0 = r * per
        ncs = min(per, nchunk - c0)
        hdst = apx(hi_h, 0, 128, (dst_c0 + c0) * W + dst_j * 128, [[W, ncs], [1, 128]])
        ldst = apx(lo_h, 0, 128, (dst_c0 + c0) * W + dst_j * 128, [[W, ncs], [1, 128]])
        psrc = apx(pb, 0, 128, 0, [[1, ncs * 128]])
        k.vec(lambda e, hdst=hdst, psrc=psrc: e.tensor_copy(hdst, psrc))
        k.vec(lambda e, ldst=ldst, psrc=psrc, hdst=hdst: e.tensor_tensor(
            ldst, psrc, hdst, Alu.subtract))
        banks.release(b, 'vector')


def mm3(k, pb_ap, wh, wl, xh, xl, first, last=False):
    k.te(lambda e: e.matmul(pb_ap, wh, xh, start=first, stop=False))
    k.te(lambda e: e.matmul(pb_ap, wh, xl, start=False, stop=False))
    k.te(lambda e: e.matmul(pb_ap, wl, xh, start=False, stop=last))


def mm2(k, pb_ap, w, xh, xl, first, last=False):
    k.te(lambda e: e.matmul(pb_ap, w, xh, start=first, stop=False))
    k.te(lambda e: e.matmul(pb_ap, w, xl, start=False, stop=last))


def dram_ap(h, off, dims):
    return bass.AP(h, off, dims)


class Ring:

    _uid = 0

    def __init__(self, k, buf_h, nslots=2, q2=None):
        self.k = k
        self.h = buf_h
        self.n = nslots
        self.rel_pe = [0] * nslots
        Ring._uid += 1
        self.sems = [f'rg{Ring._uid}_{i}' for i in range(nslots)]
        self.sems2 = [f'rh{Ring._uid}_{i}' for i in range(nslots)]
        self.ws = [0] * nslots
        self.ws2 = [0] * nslots
        self.q2 = q2 or 'sync'

    def load(self, slot, src_ap, off=0, n=None):
        self.k.wait('sync', 'pe', self.rel_pe[slot])
        self.ws[slot] = self.k.dma('sync', self.sems[slot],
                                   chx(self.h, slot, off=off, n=n), src_ap)

    def load2(self, slot, dst_ap, src_ap):
        if self.q2 != 'sync':
            self.k.wait(self.q2, 'pe', self.rel_pe[slot])
            self.ws2[slot] = self.k.dma(self.q2, self.sems2[slot], dst_ap, src_ap)
        else:
            self.ws[slot] = self.k.dma('sync', self.sems[slot], dst_ap, src_ap)

    def use(self, slot):
        self.k.wait('tensor', self.sems[slot], self.ws[slot])
        if self.ws2[slot]:
            self.k.wait('tensor', self.sems2[slot], self.ws2[slot])

    def done(self, slot):
        self.rel_pe[slot] = self.k.sig('tensor', 'pe')


class Builder:

    def __init__(self, unit=None, n_tiles=8):
        nc = bass.Bass('TRN2', target_bir_lowering=False)
        self.nc = nc
        self.k = K(nc)
        self.sb = SB(nc)
        self.banks = Banks(nc, self.k)
        self.unit = unit
        self.n_tiles = n_tiles
        self.d = {}
        self.s = {}

    def dram(self, name, shape, kind='ExternalInput'):
        if name in self.d:
            return self.d[name]
        h = self.nc.dram_tensor(name, shape, F32, kind=kind)
        self.d[name] = h
        return h

    def dram2(self, name, shape, kind='Internal'):
        h = self.nc.dram_tensor(name, shape, BF16, kind=kind)
        self.d[name] = h
        return h


def p2_declare(B):
    nc, k, sb = B.nc, B.k, B.sb
    d = B.dram
    d('wqt', [D, D]); d('wkt', [D, D]); d('wvt', [D, D])
    d('pev', [128, NH * 8 * 8])
    d('pew', [128, NH * 11])
    d('w1r', [32, 128, 8, 128]); d('w2r', [32, 128, 8, 128])
    d('h1r', [32, 128, 32, 128]); d('h2r', [32, 128, 32, 128])
    d('w3r', [8, 128, 32, 128])
    d('s1f', [128, 32]); d('s2f', [128, 32]); d('hs1f', [128, 32]); d('hs2f', [128, 32])
    d('s3f', [128, 8])
    d('iden', [128, 128]); d('maskl1', [128, 128]); d('oobm', [128, 512])
    d('sel4', [128, 4]); d('delta4', [128, 128])
    import os as _os
    d('y0', [4096, D], kind=('ExternalInput' if B.unit == 'p2' else
                             ('ExternalOutput' if _os.environ.get('K_DBG_Y0') else 'Internal')))
    if B.unit == 'p2':
        d('qf_d', [128, 8 * 512], kind='ExternalOutput')
        d('kf_d', [128, 8 * 512], kind='ExternalOutput')
        d('vp_d', [128, 4 * 1040], kind='ExternalOutput')
        d('qaug_d', [128, 4 * 512], kind='ExternalOutput')
        d('epem_d', [128, 4 * 512], kind='ExternalOutput')
        d('attn_d', [128, 4 * 1024], kind='ExternalOutput')
        d('peob_d', [128, 64], kind='ExternalOutput')
    d('yout', [4096, D], kind='ExternalOutput')
    B.dram2('wqh', [D, D]); B.dram2('wql', [D, D])
    B.dram2('wkh', [D, D]); B.dram2('wkl', [D, D])
    B.dram2('wvh', [D, D]); B.dram2('wvl', [D, D])
    B.dram2('w1b', [32, 128, 8, 128]); B.dram2('w2b', [32, 128, 8, 128])
    B.dram2('h1b', [32, 128, 32, 128]); B.dram2('h2b', [32, 128, 32, 128])
    B.dram2('w3b', [8, 128, 32, 128])
    s = B.s
    for nm, shape in [('IDEN', [128, 128]), ('MASKL1', [128, 128]), ('OOBM', [128, 512]),
                      ('SEL4', [128, 4]), ('DELTA4', [128, 128]), ('PEV', [128, NH * 64]),
                      ('PEW', [128, NH * 11]), ('S1', [128, 32]), ('S2', [128, 32]),
                      ('HS1', [128, 32]), ('HS2', [128, 32]), ('S3', [128, 8])]:
        s[nm] = sb.t4(nm, shape)


def p2_load_consts(B):
    k, s, d = B.k, B.s, B.d
    pairs = [('IDEN', 'iden'), ('MASKL1', 'maskl1'), ('OOBM', 'oobm'), ('SEL4', 'sel4'),
             ('DELTA4', 'delta4'), ('PEV', 'pev'), ('PEW', 'pew'), ('S1', 's1f'),
             ('S2', 's2f'), ('HS1', 'hs1f'), ('HS2', 'hs2f'), ('S3', 's3f')]
    for nm, dn in pairs:
        k.dma('scalar', 'xl', apx(s[nm]), d[dn].ap())


def p2_prep_qkv(B):
    nc, k, sb, d = B.nc, B.k, B.sb, B.d
    for nm in ('pi0', 'pi1', 'po0', 'po1'):
        k.wait('gpsimd', nm)
    sb.mark()
    stg4 = sb.t4('pstg4', [128, 2, 1024])
    stgh = sb.t2('psth', [128, 2, 1024])
    stgl = sb.t2('pstl', [128, 2, 1024])
    rel = [0, 0]
    wrel = [0, 0]
    pin = [0, 0]
    for wi, (src, dsth, dstl) in enumerate([('wqt', 'wqh', 'wql'),
                                            ('wkt', 'wkh', 'wkl'),
                                            ('wvt', 'wvh', 'wvl')]):
        for kc in range(8):
            sl = kc % 2
            k.wait('gpsimd', 'dv', rel[sl])
            pin[sl] = k.dma('gpsimd', f'pi{sl}', chx(stg4, sl),
                            dram_ap(d[src], kc * 128 * D, [[1, 128 * D]]))
            k.wait('vector', f'pi{sl}', pin[sl])
            k.wait('vector', f'po{sl}', wrel[sl])
            k.vec(lambda e, sl=sl: e.tensor_copy(chx(stgh, sl), chx(stg4, sl)))
            k.vec(lambda e, sl=sl: e.tensor_tensor(chx(stgl, sl), chx(stg4, sl),
                                                   chx(stgh, sl), Alu.subtract))
            rel[sl] = k.sig('vector', 'dv')
            k.wait('gpsimd', 'dv', rel[sl])
            k.dma('gpsimd', f'po{sl}', dram_ap(d[dsth], kc * 128 * D, [[1, 128 * D]]),
                  chx(stgh, sl))
            wrel[sl] = k.dma('gpsimd', f'po{sl}',
                             dram_ap(d[dstl], kc * 128 * D, [[1, 128 * D]]),
                             chx(stgl, sl))
    for nm in ('po0', 'po1'):
        k.wait('gpsimd', nm)
    k.sig('gpsimd', 'gp')
    sb.release()


def p2_prep_tern(B):
    nc, k, sb, d = B.nc, B.k, B.sb, B.d
    sb.mark()
    pad = 224992 - ((sb.top + 31) // 32 * 32)
    assert pad >= 0
    if pad:
        sb.t4('tpad', [128, pad // 4])
    stgb = sb.t2('pstb', [128, 2, 1024])
    brel = [0, 0]
    bin_ = [0, 0]
    for src, dst in [('w1r', 'w1b'), ('w2r', 'w2b'),
                     ('h1r', 'h1b'), ('h2r', 'h2b'), ('w3r', 'w3b')]:
        tot = 1
        for x in d[src].shape:
            tot *= x
        ns = tot // (128 * 1024)
        for i in range(ns):
            sl = i % 2
            k.wait('gpsimd', f'bo{sl}', brel[sl])
            bin_[sl] = k.dma('gpsimd', f'bi{sl}', chx(stgb, sl),
                             dram_ap(d[src], i * 128 * 1024, [[1, 128 * 1024]]))
            k.wait('gpsimd', f'bi{sl}', bin_[sl])
            brel[sl] = k.dma('gpsimd', f'bo{sl}',
                             dram_ap(d[dst], i * 128 * 1024, [[1, 128 * 1024]]),
                             chx(stgb, sl))
    for nm in ('bo0', 'bo1'):
        k.wait('gpsimd', nm)
    k.sig('gpsimd', 'gp')
    sb.release()


def p2_alloc(B):
    sb, s = B.sb, B.s
    s['Y0'] = sb.t4('Y0', [128, 4, 1024])
    s['SQ'] = sb.t4('SQ', [128, 1024])
    s['STATS'] = sb.t4('STATS', [128, 12])
    s['XNH'] = sb.t2('XNH', [128, 8, 512])
    s['XNL'] = sb.t2('XNL', [128, 8, 512])
    s['QKVW'] = sb.t2('QKVW', [128, 2, 2048])
    s['W12'] = sb.t2('W12', [128, 2, 2048])
    s['HW'] = sb.t2('HW', [128, 2, 4096])
    s['W3W'] = sb.t2('W3W', [128, 2, 4096])
    sb.mark()
    s['QF'] = sb.t4('QF', [128, 8, 512])
    s['KF'] = sb.t4('KF', [128, 8, 512])
    s['VP'] = sb.t4('VP', [128, 4, 1040])
    s['QAUG'] = sb.t4('QAUG', [128, 4, 512])
    s['EPEM'] = sb.t4('EPEM', [128, 4, 512])
    s['PP'] = sb.t4('PP', [128, 2, 512])
    s['SATT'] = sb.t4('SATT', [128, 2, 512])
    s['DENB'] = sb.t4('DENB', [128, 4])
    s['PEOB'] = sb.t4('PEOB', [128, 4, 16])
    sb.mark()
    s['XN4'] = sb.t4('XN4', [128, 4, 1024])
    sb.release()
    s['ATTN'] = sb.t4('ATTN', [128, 4, 1024])
    sb.release()
    s['HH'] = sb.t2('HH', [128, 32, 256])
    s['HL'] = sb.t2('HL', [128, 32, 256])
    s['GH'] = sb.t2('GH', [128, 32, 256])
    s['GL'] = sb.t2('GL', [128, 32, 256])
    s['W3F'] = sb.t4('W3F', [128, 8, 256])
    s['HF'] = sb.t4('HF', [128, 256])
    s['GF'] = sb.t4('GF', [128, 256])
    pass


def p2_init(B):
    k, s = B.k, B.s
    if 'CONST' not in s:
        s['CONST'] = B.sb.t4('CONST', [128, 4])
        k.vec(lambda e: e.memset(apx(s['CONST'], 0, 128, 0, [[1, 1]]), EPS))
        B.nc.const_aps.register(F32, EPS, apx(s['CONST'], 0, 128, 0, [[1, 1]]))
    k.vec(lambda e: e.memset(apx(s['QAUG']), 0.0))
    k.vec(lambda e: e.memset(apx(s['EPEM']), 0.0))
    k.vec(lambda e: e.memset(apx(s['VP'], 0, 128, 64, [[1040, 4], [65, 16], [1, 1]]), 1.0))
    for pb in B.banks.t:
        k.vec(lambda e, pb=pb: e.memset(apx(pb), 0.0))
    dv = k.sig('vector', 'dv')
    for i in range(8):
        B.banks.rel[i] = (dv, 0)


def p2_tile(B, t, rings):
    k, s, dd, banks = B.k, B.s, B.d, B.banks
    ringq, ring12, ringh, ring3 = rings
    IDENAP = apx(s['IDEN'])
    k.wait('scalar', 'go')
    src = dram_ap(dd['y0'], t * 512 * D, [[D, 128], [128 * D, 4], [1, D]])
    yl = k.dma('scalar', 'yl', apx(s['Y0']), src)
    k.wait('vector', 'yl', yl)
    ln_tiles(k, s['Y0'], s['XN4'], [(j * 1024, j * 1024) for j in range(4)],
             s['STATS'], s['SQ'])
    dv = k.sig('vector', 'dv')
    k.wait('tensor', 'dv', dv)
    for j in range(4):
        transpose_to_fm_split(k, banks, [(2 * j) % 8, (2 * j + 1) % 8], s['XN4'],
                              j * 1024, s['XNH'], s['XNL'], 0, j, IDENAP)
    k.wait('tensor', 'dv')
    for proj, (wh, wl, dsth) in enumerate([('wqh', 'wql', 'QF'), ('wkh', 'wkl', 'KF')]):
        for kc in range(8):
            sl = kc % 2
            ringq.load(sl, dram_ap(dd[wh], kc * 128 * D, [[1, 128 * D]]), off=0, n=1024)
            ringq.load2(sl, chx(s['QKVW'], sl, off=1024, n=1024),
                        dram_ap(dd[wl], kc * 128 * D, [[1, 128 * D]]))
            ringq.use(sl)
            for mc in range(8):
                pb = banks.acquire(mc) if kc == 0 else banks.t[mc]
                mm3(k, apx(pb),
                    chx(s['QKVW'], sl, off=mc * 128, n=128),
                    chx(s['QKVW'], sl, off=1024 + mc * 128, n=128),
                    chx(s['XNH'], kc), chx(s['XNL'], kc),
                    first=(kc == 0), last=(kc == 7))
            ringq.done(sl)
        pe = k.sig('tensor', 'pe')
        k.wait('vector', 'pe', pe)
        for mc in range(8):
            k.vec(lambda e, mc=mc, dsth=dsth: e.tensor_copy(
                chx(s[dsth], mc), apx(banks.t[mc])))
            banks.release(mc, 'vector')
    for kc in range(8):
        sl = kc % 2
        ringq.load(sl, dram_ap(dd['wvh'], kc * 128 * D, [[1, 128 * D]]), off=0, n=1024)
        ringq.load2(sl, chx(s['QKVW'], sl, off=1024, n=1024),
                    dram_ap(dd['wvl'], kc * 128 * D, [[1, 128 * D]]))
        ringq.use(sl)
        for tc in range(4):
            for half in range(2):
                b = tc * 2 + half
                pb = banks.acquire(b) if kc == 0 else banks.t[b]
                rh = chx(s['QKVW'], sl, off=half * 512, n=512)
                rl = chx(s['QKVW'], sl, off=1024 + half * 512, n=512)
                xh = chx(s['XNH'], kc, off=tc * 128, n=128)
                xxl = chx(s['XNL'], kc, off=tc * 128, n=128)
                k.te(lambda e, pb=pb, xh=xh, rh=rh, f=(kc == 0): e.matmul(
                    apx(pb), xh, rh, start=f, stop=False))
                k.te(lambda e, pb=pb, xxl=xxl, rh=rh: e.matmul(
                    apx(pb), xxl, rh, start=False, stop=False))
                k.te(lambda e, pb=pb, xh=xh, rl=rl, l=(kc == 7): e.matmul(
                    apx(pb), xh, rl, start=False, stop=l))
        ringq.done(sl)
    pe = k.sig('tensor', 'pe')
    k.wait('vector', 'pe', pe)
    k.vec(lambda e: e.memset(apx(s['VP'], 0, 128, 64, [[1040, 4], [65, 16], [1, 1]]), 1.0))
    k.vec(lambda e: e.memset(apx(s['EPEM']), 0.0))
    for tc in range(4):
        for half in range(2):
            b = tc * 2 + half
            dst = apx(s['VP'], 0, 128, tc * 1040 + half * 8 * 65, [[65, 8], [1, 64]])
            k.vec(lambda e, b=b, dst=dst: e.tensor_copy(
                dst, apx(banks.t[b], 0, 128, 0, [[64, 8], [1, 64]])))
            banks.release(b, 'vector')
    k.wait('tensor', 'dv')
    for hg in range(4):
        pb = banks.acquire(hg)
        for hs in range(4):
            h = hg * 4 + hs
            for b8 in range(8):
                out = apx(pb, 32 * hs, 8, b8 * 64, [[1, 64]])
                lhs = apx(s['PEV'], (h % 2) * 64, 64, h * 64 + b8 * 8, [[1, 8]])
                rhs = apx(s['QF'], (h % 2) * 64, 64, (h // 2) * 512 + b8, [[8, 64]])
                tp = ((h % 2) * 64, 32 * hs)
                k.te(lambda e, out=out, lhs=lhs, rhs=rhs, f=(b8 == 0), tp=tp,
                     l=(b8 == 7): e.matmul(
                    out, lhs, rhs, start=f, stop=l, tile_position=tp))
        pe = k.sig('tensor', 'pe')
        k.wait('vector', 'pe', pe)
        k.vec(lambda e, hg=hg, pb=pb: e.tensor_copy(chx(s['QAUG'], hg), apx(pb)))
        banks.release(hg, 'vector')
    for hg in range(4):
        pb = banks.acquire(4 + hg)
        for hs in range(4):
            h = hg * 4 + hs
            out = apx(pb, 32 * hs, 11, 0, [[1, 512]])
            lhs = apx(s['PEW'], (h % 2) * 64, 64, h * 11, [[1, 11]])
            rhs = chx(s['QF'], h // 2, (h % 2) * 64, 64)
            tp = ((h % 2) * 64, 32 * hs)
            k.te(lambda e, out=out, lhs=lhs, rhs=rhs, tp=tp: e.matmul(
                out, lhs, rhs, start=True, stop=True, tile_position=tp))
        pe = k.sig('tensor', 'pe')
        k.wait('scalar', 'pe', pe)
        for hs in range(4):
            k.act(lambda e, hg=hg, hs=hs, pb=pb: e.activation(
                apx(s['EPEM'], 32 * hs, 11, hg * 512, [[1, 512]]),
                apx(pb, 32 * hs, 11, 0, [[1, 512]]),
                AF.Exp, bias=0.0, scale=1.0 / 32.0))
        banks.release(4 + hg, 'scalar')
    ac = k.sig('scalar', 'ac')
    k.wait('vector', 'ac', ac)
    for hg in range(4):
        k.vec(lambda e, hg=hg: e.tensor_tensor(
            chx(s['EPEM'], hg), chx(s['EPEM'], hg), apx(s['OOBM']), Alu.mult))
    dv_epem = k.sig('vector', 'dv')
    k.wait('tensor', 'dv', dv_epem)
    pb = banks.acquire(0)
    for grp in range(4):
        for hg in range(4):
            out = apx(pb, 0, 128, grp * 16 + hg * 4, [[1, 4]])
            lhs = apx(s['EPEM'], 0, 128, hg * 512 + grp * 128, [[1, 128]])
            k.te(lambda e, out=out, lhs=lhs, f=(grp == 0 and hg == 0),
                 l=(grp == 3 and hg == 3): e.matmul(
                out, lhs, apx(s['SEL4']), start=f, stop=l))
    pe = k.sig('tensor', 'pe')
    k.wait('vector', 'pe', pe)
    k.vec(lambda e, pb=pb: e.tensor_copy(apx(s['PEOB']), apx(pb, 0, 128, 0, [[1, 64]])))
    banks.release(0, 'vector')
    pp_rel = [0, 0]
    satt_rel = [0, 0]
    pend2 = [None]

    def emit_pv2(grp, hg, slot, acv):
        vb = banks.acquire(4 + hg)
        k.wait('tensor', 'ac', acv)
        for hs in range(4):
            h = hg * 4 + hs
            out = apx(vb, 0, 128, hs * 65, [[1, 65]])
            lhs = apx(s['PP'], 0, 128, slot * 512 + hs * 128, [[1, 128]])
            rhs = apx(s['VP'], 0, 128, grp * 1040 + h * 65, [[1, 65]])
            k.te(lambda e, out=out, lhs=lhs, rhs=rhs, f=(hs == 0),
                 l=(hs == 3): e.matmul(out, lhs, rhs, start=f, stop=l))
        pp_rel[slot] = k.sig('tensor', 'pe')
        pe = pp_rel[slot]
        k.wait('vector', 'pe', pe)
        k.vec(lambda e, vb=vb, grp=grp, hg=hg: e.tensor_tensor(
            apx(s['DENB']), apx(vb, 0, 128, 64, [[65, 4]]),
            apx(s['PEOB'], 0, 128, grp * 16 + hg * 4, [[1, 4]]), Alu.add))
        k.vec(lambda e: e.reciprocal(apx(s['DENB']), apx(s['DENB'])))
        dst = apx(s['ATTN'], 0, 128, grp * 1024 + hg * 256, [[1, 256]])
        k.vec(lambda e, vb=vb, dst=dst: e.tensor_tensor(
            dst, apx(vb, 0, 128, 0, [[65, 4], [1, 64]]),
            apx(s['DENB'], 0, 128, 0, [[1, 4], [0, 64]]), Alu.mult))
        banks.release(4 + hg, 'vector')

    for grp in range(4):
        for hg in range(4):
            slot = (grp * 4 + hg) % 2
            gb = banks.acquire(hg)
            for hs in range(4):
                h = hg * 4 + hs
                out = apx(gb, 0, 128, hs * 128, [[1, 128]])
                lhs = apx(s['KF'], (h % 2) * 64, 64, (h // 2) * 512 + grp * 128, [[1, 128]])
                rhs = apx(s['QF'], (h % 2) * 64, 64, (h // 2) * 512 + grp * 128, [[1, 128]])
                k.te(lambda e, out=out, lhs=lhs, rhs=rhs, f=(hs == 0): e.matmul(
                    out, lhs, rhs, start=f, stop=False))
                lhs2 = apx(s['DELTA4'], 32 * hs, 8)
                rhs2 = apx(s['QAUG'], 32 * hs, 8, hg * 512 + grp * 16, [[1, 16], [64, 8]])
                k.te(lambda e, out=out, lhs2=lhs2, rhs2=rhs2, tp=(32 * hs, 0),
                     l=(hs == 3): e.matmul(
                    out, lhs2, rhs2, start=False, stop=l, tile_position=tp))
            pe = k.sig('tensor', 'pe')
            if pend2[0] is not None:
                emit_pv2(*pend2[0])
            k.wait('vector', 'pe', pe)
            k.wait('vector', 'ac', satt_rel[slot])
            maskrep = apx(s['MASKL1'], 0, 128, 0, [[0, 4], [1, 128]])
            k.vec(lambda e, gb=gb, slot=slot, maskrep=maskrep: e.scalar_tensor_tensor(
                chx(s['SATT'], slot), apx(gb), 1.0 / 32.0, maskrep, Alu.mult, Alu.add))
            banks.release(hg, 'vector')
            dv = k.sig('vector', 'dv')
            k.wait('scalar', 'dv', dv)
            k.wait('scalar', 'pe', pp_rel[slot])
            k.act(lambda e, slot=slot: e.activation(
                chx(s['PP'], slot), chx(s['SATT'], slot), AF.Exp, bias=0.0, scale=1.0))
            satt_rel[slot] = k.sig('scalar', 'ac')
            pend2[0] = (grp, hg, slot, satt_rel[slot])
    emit_pv2(*pend2[0])
    if B.unit == 'p2' and t == 0:
        dv = k.sig('vector', 'dv')
        k.wait('gpsimd', 'dv', dv)
        for nm, hh in [('qf_d', s['QF']), ('kf_d', s['KF']), ('vp_d', s['VP']),
                       ('qaug_d', s['QAUG']), ('epem_d', s['EPEM']),
                       ('attn_d', s['ATTN']), ('peob_d', s['PEOB'])]:
            k.dma('gpsimd', 'go', dd[nm].ap(), apx(hh))
        k.wait('vector', 'go')
        k.wait('tensor', 'go')
        k.wait('scalar', 'go')
    for j in range(4):
        k.vec(lambda e, j=j: e.tensor_add(chx(s['Y0'], j), chx(s['Y0'], j),
                                          chx(s['ATTN'], j)))
    dv = k.sig('vector', 'dv')
    k.wait('tensor', 'dv', dv)
    uprev = [0]
    for sub in range(2):
        off = sub * 256
        for mc in range(32):
            sl = mc % 2
            ring12.load(sl, dram_ap(dd['w1b'], mc * 128 * 1024, [[1, 128 * 1024]]),
                        off=0, n=1024)
            ring12.load2(sl, chx(s['W12'], sl, off=1024, n=1024),
                         dram_ap(dd['w2b'], mc * 128 * 1024, [[1, 128 * 1024]]))
            ring12.use(sl)
            b1 = (mc % 2) * 2
            b2 = b1 + 1
            p1 = banks.acquire(b1)
            p2_ = banks.acquire(b2)
            for kc in range(8):
                w1 = chx(s['W12'], sl, off=kc * 128, n=128)
                w2 = chx(s['W12'], sl, off=1024 + kc * 128, n=128)
                xh = chx(s['XNH'], kc, off=off, n=256)
                xxl = chx(s['XNL'], kc, off=off, n=256)
                mm2(k, apx(p1, 0, 128, 0, [[1, 256]]), w1, xh, xxl,
                    first=(kc == 0), last=(kc == 7))
                mm2(k, apx(p2_, 0, 128, 0, [[1, 256]]), w2, xh, xxl,
                    first=(kc == 0), last=(kc == 7))
            ring12.done(sl)
            pe = k.sig('tensor', 'pe')
            k.wait('scalar', 'pe', pe)
            k.wait('scalar', 'dv', uprev[0])
            k.act(lambda e, p1=p1, mc=mc: e.activation(
                apx(s['HF']), apx(p1, 0, 128, 0, [[1, 256]]), AF.Silu,
                bias=0.0, scale=apx(s['S1'], 0, 128, mc, [[1, 1]])))
            banks.release(b1, 'scalar')
            ac = k.sig('scalar', 'ac')
            k.wait('vector', 'ac', ac)
            k.vec(lambda e, p2_=p2_, mc=mc: e.scalar_tensor_tensor(
                apx(s['GF']), apx(p2_, 0, 128, 0, [[1, 256]]),
                apx(s['S2'], 0, 128, mc, [[1, 1]]), apx(s['HF']), Alu.mult, Alu.mult))
            banks.release(b2, 'vector')
            k.vec(lambda e, mc=mc: e.tensor_copy(chx(s['HH'], mc), apx(s['GF'])))
            k.vec(lambda e, mc=mc: e.tensor_tensor(chx(s['HL'], mc), apx(s['GF']),
                                                   chx(s['HH'], mc), Alu.subtract))
            uprev[0] = k.sig('vector', 'dv')
        k.wait('tensor', 'dv')
        for stage, (wsrc, hin_h, hin_l, gout_h, gout_l, scl) in enumerate([
                ('h1b', 'HH', 'HL', 'GH', 'GL', 'HS1'),
                ('h2b', 'GH', 'GL', 'HH', 'HL', 'HS2')]):
            for mc in range(32):
                sl = mc % 2
                ringh.load(sl, dram_ap(dd[wsrc], mc * 128 * 4096, [[1, 128 * 4096]]))
                ringh.use(sl)
                b = 4 + (mc % 4)
                pb = banks.acquire(b)
                for kc in range(32):
                    w = chx(s['HW'], sl, off=kc * 128, n=128)
                    mm2(k, apx(pb, 0, 128, 0, [[1, 256]]), w,
                        chx(s[hin_h], kc), chx(s[hin_l], kc),
                        first=(kc == 0), last=(kc == 31))
                ringh.done(sl)
                pe = k.sig('tensor', 'pe')
                k.wait('scalar', 'pe', pe)
                k.wait('scalar', 'dv', uprev[0])
                k.act(lambda e, pb=pb, mc=mc, scl=scl: e.activation(
                    apx(s['GF']), apx(pb, 0, 128, 0, [[1, 256]]), AF.Gelu,
                    bias=0.0, scale=apx(s[scl], 0, 128, mc, [[1, 1]])))
                banks.release(b, 'scalar')
                ac = k.sig('scalar', 'ac')
                k.wait('vector', 'ac', ac)
                k.vec(lambda e, mc=mc, gout_h=gout_h: e.tensor_copy(
                    chx(s[gout_h], mc), apx(s['GF'])))
                k.vec(lambda e, mc=mc, gout_h=gout_h, gout_l=gout_l: e.tensor_tensor(
                    chx(s[gout_l], mc), apx(s['GF']), chx(s[gout_h], mc), Alu.subtract))
                uprev[0] = k.sig('vector', 'dv')
            k.wait('tensor', 'dv')
        for mc in range(8):
            sl = mc % 2
            ring3.load(sl, dram_ap(dd['w3b'], mc * 128 * 4096, [[1, 128 * 4096]]))
            ring3.use(sl)
            b = mc % 4
            pb = banks.acquire(b)
            for kc in range(32):
                w = chx(s['W3W'], sl, off=kc * 128, n=128)
                mm2(k, apx(pb, 0, 128, 0, [[1, 256]]), w,
                    chx(s['HH'], kc), chx(s['HL'], kc),
                    first=(kc == 0), last=(kc == 31))
            ring3.done(sl)
            pe = k.sig('tensor', 'pe')
            k.wait('vector', 'pe', pe)
            k.vec(lambda e, pb=pb, mc=mc: e.tensor_scalar(
                chx(s['W3F'], mc), apx(pb, 0, 128, 0, [[1, 256]]),
                apx(s['S3'], 0, 128, mc, [[1, 1]]), 0.0, Alu.mult, Alu.add))
            banks.release(b, 'vector')
        dv = k.sig('vector', 'dv')
        k.wait('tensor', 'dv', dv)
        for jj in range(2):
            j = sub * 2 + jj
            for half in range(2):
                b = 4 + jj * 2 + half
                pb = banks.acquire(b)
                for q in range(4):
                    c = half * 4 + q
                    src = apx(s['W3F'], 0, 128, c * 256 + jj * 128, [[1, 128]])
                    k.te(lambda e, pb=pb, q=q, src=src: e.matmul(
                        apx(pb, 0, 128, q * 128, [[1, 128]]), src, IDENAP,
                        start=(q == 0), stop=(q == 3), is_transpose=True))
            pe = k.sig('tensor', 'pe')
            k.wait('vector', 'pe', pe)
            for half in range(2):
                b = 4 + jj * 2 + half
                dst = apx(s['Y0'], 0, 128, j * 1024 + half * 512, [[1, 512]])
                k.vec(lambda e, dst=dst, b=b: e.tensor_add(dst, dst, apx(banks.t[b])))
                banks.release(b, 'vector')
    dv = k.sig('vector', 'dv')
    k.wait('gpsimd', 'dv', dv)
    dstd = dram_ap(dd['yout'], t * 512 * D, [[D, 128], [128 * D, 4], [1, D]])
    k.dma('gpsimd', 'go', dstd, apx(s['Y0']))


def p2_finish(B):
    B.k.wait('gpsimd', 'go')


def p2_run(B):
    k, s = B.k, B.s
    rings = (Ring(k, s['QKVW']), Ring(k, s['W12']), Ring(k, s['HW']), Ring(k, s['W3W']))
    for t in range(B.n_tiles):
        p2_tile(B, t, rings)
    p2_finish(B)



def p0_declare(B):
    nc, d = B.nc, B.dram
    if B.unit == 'p0':
        d('y0', [4096, D], kind='ExternalOutput')
    d('iden', [128, 128])
    d('xown', [NB, 640, D])
    d('xfull', [32768, D])
    d('qkvw', [D, 3 * D]); d('qkvbf', [128, 24])
    d('rqw', [D, 256]); d('rqbf', [128, 2])
    d('rkvw', [D, 512]); d('rkvbf', [128, 4])
    d('routw', [256, D])
    d('outw', [D, D])
    d('w0t', [D, D]); d('alpha1', [1, 1])
    d('glrep', [128, D]); d('grrep', [128, D])
    d('mkab', [128, 512]); d('mkab0', [128, 512])
    d('mklat', [128, 512])
    d('routbr', [1, D]); d('outbr', [1, D]); d('fbiasr', [1, D]); d('vbr', [1, D])
    B.dram2('qw_h', [D, D]); B.dram2('qw_l', [D, D])
    B.dram2('kw_h', [D, D]); B.dram2('kw_l', [D, D])
    B.dram2('vw_h', [D, D]); B.dram2('vw_l', [D, D])
    B.dram2('ow_h', [D, D]); B.dram2('ow_l', [D, D])
    B.dram2('rq_h', [D, 256]); B.dram2('rq_l', [D, 256])
    d('pool_scr', [512, D], kind='Internal')


def p0_alloc_consts(B):
    sb, s = B.sb, B.s
    for nm, shape in [('QKVB', [128, 24]), ('RQB', [128, 2]),
                      ('RKVB', [128, 4]), ('VBT', [128, D]),
                      ('RBT', [128, D]), ('CBT', [128, D]),
                      ('GLREP', [128, D]), ('GRREP', [128, D]),
                      ('MKAB', [128, 512]), ('MKAB0', [128, 512]),
                      ('MKLAT', [128, 512]),
                      ('ALF', [128, 2])]:
        s[nm] = sb.t4(nm, shape)


def p0_load_consts(B):
    k, s, d = B.k, B.s, B.d
    for nm, dn in [('QKVB', 'qkvbf'), ('RQB', 'rqbf'), ('RKVB', 'rkvbf'),
                   ('GLREP', 'glrep'), ('GRREP', 'grrep'),
                   ('MKAB', 'mkab'), ('MKAB0', 'mkab0'), ('MKLAT', 'mklat')]:
        k.dma('scalar', 'xl', apx(s[nm]), d[dn].ap())
    k.dma('scalar', 'xl', apx(s['VBT']), dram_ap(d['vbr'], 0, [[0, 128], [1, D]]))
    k.dma('scalar', 'xl', apx(s['RBT']), dram_ap(d['routbr'], 0, [[0, 128], [1, D]]))
    k.dma('scalar', 'xl', apx(s['CBT']), dram_ap(d['outbr'], 0, [[0, 128], [1, D]]))


def p0_bias_init(B):
    k, s, d = B.k, B.s, B.d
    k.wait('scalar', 'pe')
    fb = k.dma('scalar', 'ye', apx(s['SQ0']), dram_ap(d['fbiasr'], 0, [[0, 128], [1, D]]))
    k.wait('vector', 'ye', fb)
    k.wait('vector', 'xl')
    k.vec(lambda e: e.scalar_tensor_tensor(
        apx(s['CBT']), apx(s['SQ0']), apx(s['ALF'], 0, 128, 1, [[1, 1]]),
        apx(s['CBT']), Alu.mult, Alu.add))
    k.sig('vector', 'dv')


def p0_prep_weights(B):
    nc, k, sb, s, d = B.nc, B.k, B.sb, B.s, B.d
    sb.mark()
    stg4 = sb.t4('qstg4', [128, 2, 1024])
    stgh = sb.t2('qsth', [128, 2, 1024])
    stgl = sb.t2('qstl', [128, 2, 1024])
    asums = sb.t4('asums', [128, 10])
    onesr = sb.t4('onesr', [1, 128])
    rel = [0, 0]
    wrel = [0, 0]
    pin = [0, 0]
    jobs = []
    for kc in range(8):
        for third, (dh, dl) in enumerate([('qw_h', 'qw_l'), ('kw_h', 'kw_l'),
                                          ('vw_h', 'vw_l')]):
            jobs.append((d['qkvw'], kc * 128 * 3072 + third * 1024, 3072,
                         d[dh], kc * 128 * D, d[dl]))
    for kc in range(8):
        jobs.append((d['outw'], kc * 128 * D, D, d['ow_h'], kc * 128 * D, d['ow_l']))
    for kc in range(8):
        jobs.append((d['rqw'], kc * 128 * 256, 256, d['rq_h'], kc * 128 * 256,
                     d['rq_l']))
    for ji, (srch, soff, srow, dsth, doff, dstl) in enumerate(jobs):
        sl = ji % 2
        n = 1024 if srow != 256 else 256
        k.wait('gpsimd', 'dv', rel[sl])
        pin[sl] = k.dma('gpsimd', f'pi{sl}', chx(stg4, sl, n=n),
                        dram_ap(srch, soff, [[srow, 128], [1, n]]))
        k.wait('vector', f'pi{sl}', pin[sl])
        k.wait('vector', f'po{sl}', wrel[sl])
        k.vec(lambda e, sl=sl, n=n: e.tensor_copy(chx(stgh, sl, n=n), chx(stg4, sl, n=n)))
        k.vec(lambda e, sl=sl, n=n: e.tensor_tensor(chx(stgl, sl, n=n), chx(stg4, sl, n=n),
                                                    chx(stgh, sl, n=n), Alu.subtract))
        rel[sl] = k.sig('vector', 'dv')
        k.wait('gpsimd', 'dv', rel[sl])
        k.dma('gpsimd', f'po{sl}', dram_ap(dsth, doff, [[1, 128 * n]]), chx(stgh, sl, n=n))
        wrel[sl] = k.dma('gpsimd', f'po{sl}', dram_ap(dstl, doff, [[1, 128 * n]]),
                         chx(stgl, sl, n=n))
    for kc in range(8):
        sl = kc % 2
        k.wait('gpsimd', 'dv', rel[sl])
        pin[sl] = k.dma('gpsimd', f'pi{sl}', chx(stg4, sl),
                        dram_ap(d['w0t'], kc * 128 * D, [[1, 128 * D]]))
        k.wait('vector', f'pi{sl}', pin[sl])
        k.vec(lambda e, sl=sl, kc=kc: e.tensor_reduce(
            apx(asums, 0, 128, kc, [[1, 1]]), chx(stg4, sl), AX.X, Alu.add,
            apply_absolute_value=True))
        rel[sl] = k.sig('vector', 'dv')
    k.vec(lambda e: e.memset(apx(s['ONE1']), 1.0))
    dv = k.sig('vector', 'dv')
    k.wait('tensor', 'dv', dv)
    pb = B.banks.acquire(0)
    k.te(lambda e: e.matmul(apx(pb, 0, 1, 0, [[1, 8]]), apx(s['ONE1']),
                            apx(asums, 0, 128, 0, [[1, 8]]), start=True, stop=True))
    pe = k.sig('tensor', 'pe')
    k.wait('vector', 'pe', pe)
    k.vec(lambda e: e.reduce_sum(apx(asums, 0, 1, 8, [[1, 1]]),
                                 apx(pb, 0, 1, 0, [[1, 8]]), AX.X))
    B.banks.release(0, 'vector')
    k.vec(lambda e: e.tensor_scalar(apx(asums, 0, 1, 8, [[1, 1]]),
                                    apx(asums, 0, 1, 8, [[1, 1]]),
                                    0.7 / (1024.0 * 1024.0), 0.0, Alu.mult, Alu.add))
    dv = k.sig('vector', 'dv')
    k.wait('tensor', 'dv', dv)
    k.vec(lambda e: e.memset(apx(onesr), 1.0))
    dvx = k.sig('vector', 'dv')
    k.wait('tensor', 'dv', dvx)
    onesrow = apx(onesr)
    pb = B.banks.acquire(1)
    k.te(lambda e, onesrow=onesrow: e.matmul(apx(pb, 0, 128, 0, [[1, 1]]),
                            onesrow,
                            apx(asums, 0, 1, 8, [[1, 1]]), start=True, stop=False))
    k.on('sync', lambda e: None)
    alf_stage = apx(s['ALF'], 0, 1, 1, [[1, 1]])
    xl = k.dma('scalar', 'xl', alf_stage, d['alpha1'].ap())
    k.wait('tensor', 'xl', xl)
    k.te(lambda e, onesrow=onesrow: e.matmul(apx(pb, 0, 128, 1, [[1, 1]]),
                            onesrow,
                            alf_stage, start=False, stop=True))
    pe = k.sig('tensor', 'pe')
    k.wait('vector', 'pe', pe)
    k.vec(lambda e: e.tensor_copy(apx(s['ALF'], 0, 128, 0, [[1, 2]]),
                                  apx(pb, 0, 128, 0, [[1, 2]])))
    B.banks.release(1, 'vector')
    for kc in range(8):
        sl = kc % 2
        k.wait('gpsimd', 'dv', rel[sl])
        pin[sl] = k.dma('gpsimd', f'pi{sl}', chx(stg4, sl),
                        dram_ap(d['w0t'], kc * 128 * D, [[1, 128 * D]]))
        k.wait('vector', f'pi{sl}', pin[sl])
        k.wait('vector', f'po{sl}', wrel[sl])
        k.vec(lambda e, sl=sl: e.tensor_scalar(
            chx(stgh, sl), chx(stg4, sl), apx(s['ALF'], 0, 128, 0, [[1, 1]]), None,
            Alu.is_gt))
        k.vec(lambda e, sl=sl: e.tensor_scalar(
            chx(stgl, sl), chx(stg4, sl), -1.0, apx(s['ALF'], 0, 128, 0, [[1, 1]]),
            Alu.mult, Alu.is_gt))
        k.vec(lambda e, sl=sl, kc=kc: e.tensor_tensor(
            chx(s['TERN'], kc), chx(stgh, sl), chx(stgl, sl), Alu.subtract))
        rel[sl] = k.sig('vector', 'dv')
    for nm in ('po0', 'po1'):
        k.wait('gpsimd', nm)
    k.sig('gpsimd', 'gp')
    sb.release()


def p0_alloc(B):
    sb, s = B.sb, B.s
    s['SQ0'] = sb.t4('SQ0', [128, 1024])
    s['ST0'] = sb.t4('ST0', [128, 16])
    s['TERN'] = sb.t2('TERN', [128, 8, 1024])
    s['XFH'] = sb.t2('XFH', [128, 8, 640])
    s['XFL'] = sb.t2('XFL', [128, 8, 640])
    s['RING0'] = sb.t2('RING0', [128, 2, 2048])
    s['ROUTW'] = sb.t4('ROUTW', [128, 2, 1024])
    s['RKVF'] = sb.t4('RKVF', [128, 4, 512])
    s['RVP'] = sb.t4('RVP', [128, 16, 65])
    s['XEP'] = sb.t4('XEP', [128, 1024])
    s['T1E'] = sb.t4('T1E', [128, 1024])
    s['ONE1'] = sb.t4('ONE1', [128, 1])
    s['PL0'] = sb.t4('PL0', [128, 4, 256])
    sb.mark()
    s['PLAT'] = sb.t4('PLAT', [128, 4, 512])
    sb.release()
    s['ESL'] = sb.t4('ESL', [128, 2, 1024])
    s['LOC'] = sb.t4('LOC', [128, 4, 1024])
    s['LAT'] = sb.t4('LAT', [128, 4, 256])
    s['DEN0'] = sb.t4('DEN0', [128, 4])
    s['GATE'] = sb.t4('GATE', [128, 16, 2])
    sb.mark()
    s['X0'] = sb.t4('X0', [128, 5, 1024])
    sb.release()
    sb.mark()
    s['XT'] = sb.t4('XT', [128, 2, 4096])
    s['RQ4'] = sb.t4('RQ4', [128, 8])
    s['CORR'] = sb.t4('CORR', [2, 2])
    s['PSTG'] = sb.t4('PSTG', [2, 4, 1024])
    sb.release()
    sb.mark()
    s['POOLF'] = sb.t4('POOLF', [128, 8, 512])
    sb.mark()
    s['POOLTM'] = sb.t4('POOLTM', [128, 4, 1024])
    sb.release()
    s['RKVW'] = sb.t4('RKVW', [128, 8, 512])
    sb.release()
    sb.mark()
    s['QF0'] = sb.t4('QF0', [128, 8, 512])
    s['KF0'] = sb.t4('KF0', [128, 8, 640])
    s['VP0'] = sb.t4('VP0', [128, 5, 1040])
    s['RQF'] = sb.t4('RQF', [128, 2, 512])
    sb.release()
    s['LATF'] = sb.t4('LATF', [128, 2, 512])
    s['REM'] = sb.t4('REM', [128, 4, 1024])
    s['MIXFH'] = sb.t2('MIXFH', [128, 8, 512])
    s['MIXFL'] = sb.t2('MIXFL', [128, 8, 512])
    pass


def transpose_to_fm(k, banks, bank_ids, src_h, src_off, dst_h, dst_c0, dst_j,
                    iden_ap, nchunk=8, W=None):
    W = dst_h.shape[2] if W is None else W
    per = 4
    for r, b in enumerate(bank_ids):
        pb = banks.acquire(b)
        nq = min(per, nchunk - r * per)
        for q in range(nq):
            c = r * per + q
            k.te(lambda e, pb=pb, q=q, c=c, f=(q == 0), l=(q == nq - 1): e.matmul(
                apx(pb, 0, 128, q * 128, [[1, 128]]),
                apx(src_h, 0, 128, src_off + c * 128, [[1, 128]]),
                iden_ap, start=f, stop=l, is_transpose=True))
        pe = k.sig('tensor', 'pe')
        k.wait('vector', 'pe', pe)
        c0 = r * per
        ncs = min(per, nchunk - c0)
        dst = apx(dst_h, 0, 128, (dst_c0 + c0) * W + dst_j * 128, [[W, ncs], [1, 128]])
        k.vec(lambda e, dst=dst, pb=pb, ncs=ncs: e.tensor_copy(
            dst, apx(pb, 0, 128, 0, [[1, ncs * 128]])))
        banks.release(b, 'vector')


def p0_pha(B):
    k, s, d, banks = B.k, B.s, B.d, B.banks
    for nm in ('pi0', 'pi1', 'po0', 'po1'):
        k.wait('vector', nm)
    st = s['ST0']
    xa = [0, 0]
    rel = [0, 0]
    for g in range(64):
        sl = g % 2
        k.wait('scalar', 'dv', rel[sl])
        xa[sl] = k.dma('scalar', f'xa{sl}', chx(s['XT'], sl),
                       dram_ap(d['xfull'], g * 512 * D,
                               [[D, 128], [128 * D, 4], [1, D]]))
        k.wait('vector', f'xa{sl}', xa[sl])
        k.vec(lambda e, sl=sl: e.reduce_sum(
            apx(st, 0, 128, 0, [[1, 4]]),
            apx(s['XT'], 0, 128, sl * 4096, [[1024, 4], [1, 1024]]), AX.X))
        for t4 in range(4):
            k.vec(lambda e, sl=sl, t4=t4: e.scalar_tensor_tensor(
                apx(s['SQ0']), apx(s['XT'], 0, 128, sl * 4096 + t4 * 1024, [[1, 1024]]),
                1.0, apx(s['XT'], 0, 128, sl * 4096 + t4 * 1024, [[1, 1024]]),
                Alu.mult, Alu.mult, accum_out=apx(st, 0, 128, 4 + t4, [[1, 1]])))
        k.vec(lambda e: e.tensor_scalar(apx(st, 0, 128, 0, [[1, 4]]),
                                        apx(st, 0, 128, 0, [[1, 4]]), 1.0 / D, 0.0,
                                        Alu.mult, Alu.add))
        k.vec(lambda e: e.tensor_scalar(apx(st, 0, 128, 4, [[1, 4]]),
                                        apx(st, 0, 128, 4, [[1, 4]]), 1.0 / D, 0.0,
                                        Alu.mult, Alu.add))
        k.vec(lambda e: e.tensor_tensor(apx(st, 0, 128, 8, [[1, 4]]),
                                        apx(st, 0, 128, 0, [[1, 4]]),
                                        apx(st, 0, 128, 0, [[1, 4]]), Alu.mult))
        k.vec(lambda e: e.tensor_tensor(apx(st, 0, 128, 8, [[1, 4]]),
                                        apx(st, 0, 128, 4, [[1, 4]]),
                                        apx(st, 0, 128, 8, [[1, 4]]), Alu.subtract))
        dv = k.sig('vector', 'dv')
        k.wait('scalar', 'dv', dv)
        k.act(lambda e: e.activation(apx(st, 0, 128, 8, [[1, 4]]),
                                     apx(st, 0, 128, 8, [[1, 4]]), AF.Sqrt,
                                     bias=EPS, scale=1.0))
        ac = k.sig('scalar', 'ac')
        k.wait('vector', 'ac', ac)
        k.vec(lambda e: e.reciprocal(apx(st, 0, 128, 12, [[1, 4]]),
                                     apx(st, 0, 128, 8, [[1, 4]])))
        k.vec(lambda e: e.memset(apx(s['RQ4']), 0.0))
        k.vec(lambda e: e.tensor_copy(apx(s['RQ4'], 0, 64, 0, [[2, 4]]),
                                      apx(st, 0, 64, 12, [[1, 4]])))
        k.vec(lambda e: e.tensor_copy(apx(s['RQ4'], 64, 64, 1, [[2, 4]]),
                                      apx(st, 64, 64, 12, [[1, 4]])))
        dv = k.sig('vector', 'dv')
        k.wait('tensor', 'dv', dv)
        for t4 in range(4):
            pa = banks.acquire(2 * t4)
            pbb = banks.acquire(2 * t4 + 1)
            lhsR = apx(s['RQ4'], 0, 128, t4 * 2, [[1, 2]])
            k.te(lambda e, pa=pa, lhsR=lhsR, sl=sl, t4=t4: e.matmul(
                apx(pa, 0, 2, 0, [[1, 512]]), lhsR,
                apx(s['XT'], 0, 128, sl * 4096 + t4 * 1024, [[1, 512]]),
                start=True, stop=True))
            k.te(lambda e, pbb=pbb, lhsR=lhsR, sl=sl, t4=t4: e.matmul(
                apx(pbb, 0, 2, 0, [[1, 512]]), lhsR,
                apx(s['XT'], 0, 128, sl * 4096 + t4 * 1024 + 512, [[1, 512]]),
                start=True, stop=True))
        rel[sl] = k.sig('tensor', 'pe')
        k.wait('vector', 'pe', rel[sl])
        k.wait('vector', 'go')
        for t4 in range(4):
            pa, pbb = banks.t[2 * t4], banks.t[2 * t4 + 1]
            k.vec(lambda e, pa=pa, t4=t4: e.reduce_sum(
                apx(s['CORR'], 0, 2, 0, [[1, 1]]), apx(pa, 0, 2, 0, [[1, 512]]), AX.X))
            k.vec(lambda e, pbb=pbb, t4=t4: e.reduce_sum(
                apx(s['CORR'], 0, 2, 1, [[1, 1]]), apx(pbb, 0, 2, 0, [[1, 512]]), AX.X))
            k.vec(lambda e: e.tensor_tensor(
                apx(s['CORR'], 0, 2, 0, [[1, 1]]), apx(s['CORR'], 0, 2, 0, [[1, 1]]),
                apx(s['CORR'], 0, 2, 1, [[1, 1]]), Alu.add))
            k.vec(lambda e: e.tensor_scalar(
                apx(s['CORR'], 0, 2, 0, [[1, 1]]), apx(s['CORR'], 0, 2, 0, [[1, 1]]),
                1.0 / D, 0.0, Alu.mult, Alu.add))
            k.vec(lambda e, pa=pa, t4=t4: e.tensor_scalar(
                apx(s['PSTG'], 0, 2, t4 * 1024, [[1, 512]]), apx(pa, 0, 2, 0, [[1, 512]]),
                apx(s['CORR'], 0, 2, 0, [[1, 1]]), 1.0 / 64.0, Alu.subtract, Alu.mult))
            banks.release(2 * t4, 'vector')
            k.vec(lambda e, pbb=pbb, t4=t4: e.tensor_scalar(
                apx(s['PSTG'], 0, 2, t4 * 1024 + 512, [[1, 512]]),
                apx(pbb, 0, 2, 0, [[1, 512]]),
                apx(s['CORR'], 0, 2, 0, [[1, 1]]), 1.0 / 64.0, Alu.subtract, Alu.mult))
            banks.release(2 * t4 + 1, 'vector')
        rel[sl] = k.sig('vector', 'dv')
        k.wait('gpsimd', 'dv', rel[sl])
        k.dma('gpsimd', 'go',
              dram_ap(d['pool_scr'], g * 8 * D, [[D, 2], [2 * D, 4], [1, D]]),
              apx(s['PSTG']))
    k.wait('gpsimd', 'go')
    k.sig('gpsimd', 'gp')


def p0_rkv(B, core):
    k, s, d, banks = B.k, B.s, B.d, B.banks
    IDENAP = apx(B.s['IDEN'])
    k.wait('scalar', 'gp')
    xl = k.dma('scalar', 'yl', apx(s['POOLTM']), dram_ap(d['pool_scr'], 0, [[1, 512 * D]]))
    k.wait('vector', 'yl', xl)
    k.wait('tensor', 'yl', xl)
    for j in range(4):
        transpose_to_fm(k, banks, [2 * j, 2 * j + 1], s['POOLTM'], j * 1024,
                        s['POOLF'], 0, j, IDENAP)
    dvp = k.sig('vector', 'dv')
    k.wait('scalar', 'dv', dvp)
    wl = k.dma('scalar', 'xl', apx(s['RKVW']),
               dram_ap(d['rkvw'], 0, [[512, 128], [128 * 512, 8], [1, 512]]))
    k.wait('tensor', 'xl', wl)
    k.wait('tensor', 'dv')
    for kc in range(8):
        for m in range(4):
            pb = banks.acquire(m) if kc == 0 else banks.t[m]
            k.te(lambda e, pb=pb, kc=kc, m=m, f=(kc == 0), l=(kc == 7): e.matmul(
                apx(pb), apx(s['RKVW'], 0, 128, kc * 512 + m * 128, [[1, 128]]),
                chx(s['POOLF'], kc), start=f, stop=l))
    pe = k.sig('tensor', 'pe')
    k.wait('vector', 'pe', pe)
    for m in range(4):
        k.vec(lambda e, m=m: e.tensor_scalar(
            chx(s['RKVF'], m), apx(banks.t[m]),
            apx(s['RKVB'], 0, 128, m, [[1, 1]]), 0.0, Alu.add, Alu.add))
        banks.release(m, 'vector')
    dv = k.sig('vector', 'dv')
    k.wait('tensor', 'dv', dv)
    k.vec(lambda e: e.memset(apx(s['RVP'], 0, 128, 64, [[65, 16], [1, 1]]), 1.0))
    dv = k.sig('vector', 'dv')
    k.wait('tensor', 'dv', dv)
    for t8 in range(8):
        cc = 2 + t8 // 4
        i = t8 % 4
        pb = banks.acquire(4 + t8 % 4)
        src = apx(s['RKVF'], 0, 128, cc * 512 + i * 128, [[1, 128]])
        k.te(lambda e, pb=pb, src=src: e.matmul(
            apx(pb, 0, 128, 0, [[1, 128]]), src, IDENAP,
            start=True, stop=True, is_transpose=True))
        pe = k.sig('tensor', 'pe')
        k.wait('vector', 'pe', pe)
        for lhin in range(2):
            lh = (cc - 2) * 2 + lhin
            blk = i * 4 + lh
            dst = apx(s['RVP'], 0, 128, blk * 65, [[1, 64]])
            k.vec(lambda e, pb=pb, dst=dst, lhin=lhin: e.tensor_copy(
                dst, apx(pb, 0, 128, lhin * 64, [[1, 64]])))
        banks.release(4 + t8 % 4, 'vector')


def p0_batch(B, b, ring0):
    k, s, dd, banks = B.k, B.s, B.d, B.banks
    IDENAP = apx(s['IDEN'])
    k.wait('scalar', 'yo')
    k.wait('scalar', 'yp')
    yl = k.dma('scalar', 'yl', apx(s['X0']), dram_ap(dd['xown'], b * 640 * D,
                                                     [[D, 128], [128 * D, 5], [1, D]]))
    k.wait('vector', 'yl', yl)
    ln_tiles(k, s['X0'], s['X0'], [(j * 1024, j * 1024) for j in range(5)],
             s['ST0'], s['SQ0'])
    dv = k.sig('vector', 'dv')
    k.wait('tensor', 'dv', dv)
    for j in range(5):
        transpose_to_fm_split(k, banks, [(2 * j) % 8, (2 * j + 1) % 8], s['X0'],
                              j * 1024, s['XFH'], s['XFL'], 0, j, IDENAP)
    k.wait('tensor', 'dv')

    def stream_fm(dsth, scrh, scrl, wrow, nmch, rhs_off, rhs_n, biasT, bias_c0):
        nsp = (rhs_n + 511) // 512
        mg = max(1, 8 // nsp)
        for m0 in range(0, nmch, mg):
            mge = min(mg, nmch - m0)
            for kc in range(8):
                sl = kc % 2
                ring0.load(sl, dram_ap(dd[scrh], kc * 128 * wrow + m0 * 128,
                                       [[wrow, 128], [1, mge * 128]]), off=0, n=mge * 128)
                ring0.load2(sl, chx(s['RING0'], sl, off=1024, n=mge * 128),
                            dram_ap(dd[scrl], kc * 128 * wrow + m0 * 128,
                                    [[wrow, 128], [1, mge * 128]]))
                ring0.use(sl)
                for mi in range(mge):
                    for ns in range(nsp):
                        nn = min(512, rhs_n - ns * 512)
                        bk = mi * nsp + ns
                        pb = banks.acquire(bk) if kc == 0 else banks.t[bk]
                        xh = chx(s['XFH'], kc, off=rhs_off + ns * 512, n=nn)
                        xxl = chx(s['XFL'], kc, off=rhs_off + ns * 512, n=nn)
                        wh = chx(s['RING0'], sl, off=mi * 128, n=128)
                        wl = chx(s['RING0'], sl, off=1024 + mi * 128, n=128)
                        mm3(k, apx(pb, 0, 128, 0, [[1, nn]]), wh, wl, xh, xxl,
                            first=(kc == 0), last=(kc == 7))
                ring0.done(sl)
            pe = k.sig('tensor', 'pe')
            k.wait('vector', 'pe', pe)
            for mi in range(mge):
                for ns in range(nsp):
                    nn = min(512, rhs_n - ns * 512)
                    bk = mi * nsp + ns
                    k.vec(lambda e, mi=mi, ns=ns, nn=nn, bk=bk, m0=m0: e.tensor_scalar(
                        chx(s[dsth], m0 + mi, off=ns * 512, n=nn),
                        apx(banks.t[bk], 0, 128, 0, [[1, nn]]),
                        apx(s[biasT], 0, 128, bias_c0 + m0 + mi, [[1, 1]]), 0.0,
                        Alu.add, Alu.add))
                    banks.release(bk, 'vector')

    stream_fm('QF0', 'qw_h', 'qw_l', D, 8, 128, 512, 'QKVB', 0)
    stream_fm('KF0', 'kw_h', 'kw_l', D, 8, 0, 640, 'QKVB', 8)
    stream_fm('RQF', 'rq_h', 'rq_l', 256, 2, 128, 512, 'RQB', 0)

    k.vec(lambda e: e.memset(apx(s['VP0'], 0, 128, 64, [[1040, 5], [65, 16], [1, 1]]), 1.0))
    for rnd in range(10):
        gc, half = rnd // 2, rnd % 2
        bk = rnd % 2 + 6
        for kc in range(8):
            sl = kc % 2
            ring0.load(sl, dram_ap(dd['vw_h'], kc * 128 * D + half * 512,
                                   [[D, 128], [1, 512]]), off=0, n=512)
            ring0.load2(sl, chx(s['RING0'], sl, off=1024, n=512),
                        dram_ap(dd['vw_l'], kc * 128 * D + half * 512,
                                [[D, 128], [1, 512]]))
            ring0.use(sl)
            pb = banks.acquire(bk) if kc == 0 else banks.t[bk]
            xh = chx(s['XFH'], kc, off=gc * 128, n=128)
            xxl = chx(s['XFL'], kc, off=gc * 128, n=128)
            wh = chx(s['RING0'], sl, n=512)
            wl = chx(s['RING0'], sl, off=1024, n=512)
            k.te(lambda e, pb=pb, xh=xh, wh=wh, f=(kc == 0): e.matmul(
                apx(pb), xh, wh, start=f, stop=False))
            k.te(lambda e, pb=pb, xxl=xxl, wh=wh: e.matmul(
                apx(pb), xxl, wh, start=False, stop=False))
            k.te(lambda e, pb=pb, xh=xh, wl=wl, l=(kc == 7): e.matmul(
                apx(pb), xh, wl, start=False, stop=l))
            ring0.done(sl)
        pb = banks.t[bk]
        pe = k.sig('tensor', 'pe')
        k.wait('vector', 'pe', pe)
        dst = apx(s['VP0'], 0, 128, gc * 1040 + half * 8 * 65, [[65, 8], [1, 64]])
        vbt = apx(s['VBT'], 0, 128, half * 512, [[1, 512]])
        k.vec(lambda e, pb=pb, dst=dst, vbt=vbt: e.tensor_tensor(
            dst, apx(pb, 0, 128, 0, [[1, 512]]), vbt, Alu.add))
        banks.release(bk, 'vector')
    dv = k.sig('vector', 'dv')
    k.wait('tensor', 'dv', dv)

    esl_free = [0, 0]
    pv_sig = [0]

    def emit_pv(qb, hg, dvv):
        vb = banks.acquire(hg)
        k.wait('tensor', 'dv', dvv)
        for hs in range(4):
            h = hg * 4 + hs
            rhsV = apx(s['VP0'], 0, 128, qb * 1040 + h * 65, [[1, 65]])
            rhsV2 = apx(s['VP0'], 0, 128, (qb + 1) * 1040 + h * 65, [[1, 65]])
            lhsP = apx(s['PL0'], 0, 128, hs * 256, [[1, 128]])
            lhsP2 = apx(s['PL0'], 0, 128, hs * 256 + 128, [[1, 128]])
            k.te(lambda e, vb=vb, lhsP=lhsP, rhsV=rhsV, hs=hs, f=(hs == 0): e.matmul(
                apx(vb, 0, 128, hs * 65, [[1, 65]]), lhsP, rhsV,
                start=f, stop=False))
            k.te(lambda e, vb=vb, lhsP2=lhsP2, rhsV2=rhsV2, hs=hs, l=(hs == 3): e.matmul(
                apx(vb, 0, 128, hs * 65, [[1, 65]]), lhsP2, rhsV2,
                start=False, stop=l))
        pv_sig[0] = k.sig('tensor', 'pe')
        k.wait('vector', 'pe', pv_sig[0])
        k.vec(lambda e, vb=vb: e.tensor_scalar(
            apx(s['DEN0']), apx(vb, 0, 128, 64, [[65, 4]]), 1e-12, 0.0,
            Alu.add, Alu.add))
        k.vec(lambda e: e.reciprocal(apx(s['DEN0']), apx(s['DEN0'])))
        dst = apx(s['LOC'], 0, 128, qb * 1024 + hg * 256, [[1, 256]])
        k.vec(lambda e, vb=vb, dst=dst: e.tensor_tensor(
            dst, apx(vb, 0, 128, 0, [[65, 4], [1, 64]]),
            apx(s['DEN0'], 0, 128, 0, [[1, 4], [0, 64]]), Alu.mult))
        banks.release(hg, 'vector')

    pend = None
    for i, (qb, hg) in enumerate([(q, h) for q in range(4) for h in range(4)]):
        sl = i % 2
        gb4 = banks.acquire(4)
        gb5 = banks.acquire(5)
        for hs in range(4):
            h = hg * 4 + hs
            gb = gb4 if hs % 2 == 0 else gb5
            co = (hs // 2) * 256
            base = (h % 2) * 64
            lhsA = chx(s['KF0'], h // 2, base, 64, off=qb * 128, n=128)
            lhsB = chx(s['KF0'], h // 2, base, 64, off=qb * 128 + 128, n=128)
            rhsq = chx(s['QF0'], h // 2, base, 64, off=qb * 128, n=128)
            k.te(lambda e, gb=gb, lhsA=lhsA, rhsq=rhsq, co=co, f=(hs < 2):
                 e.matmul(apx(gb, 0, 128, co, [[1, 128]]), lhsA, rhsq,
                          start=f, stop=False))
            k.te(lambda e, gb=gb, lhsB=lhsB, rhsq=rhsq, co=co, l=(hs >= 2):
                 e.matmul(apx(gb, 0, 128, co + 128, [[1, 128]]), lhsB, rhsq,
                          start=False, stop=l))
        pe_g = k.sig('tensor', 'pe')
        if pend is not None:
            emit_pv(*pend)
        k.wait('scalar', 'pe', pe_g)
        k.wait('scalar', 'dv', esl_free[sl])
        for pi, gb in enumerate((gb4, gb5)):
            k.act(lambda e, gb=gb, sl=sl, pi=pi: e.activation(
                apx(s['ESL'], 0, 128, sl * 1024 + pi * 512, [[1, 512]]),
                apx(gb, 0, 128, 0, [[1, 512]]), AF.Exp, bias=0.0, scale=0.125))
        banks.release(4, 'scalar')
        banks.release(5, 'scalar')
        ac_i = k.sig('scalar', 'ac')
        k.wait('vector', 'ac', ac_i)
        k.wait('vector', 'pe', pv_sig[0])
        mk = 'MKAB0' if qb == 0 else 'MKAB'
        for pi in range(2):
            k.vec(lambda e, sl=sl, pi=pi, mk=mk: e.tensor_tensor(
                apx(s['PL0'], 0, 128, pi * 256, [[512, 2], [1, 256]]),
                apx(s['ESL'], 0, 128, sl * 1024 + pi * 512, [[1, 512]]),
                apx(s[mk]), Alu.mult))
        esl_free[sl] = k.sig('vector', 'dv')
        pend = (qb, hg, esl_free[sl])
    emit_pv(*pend)

    k.wait('scalar', 'dv')
    for lh in range(4):
        base = (lh % 2) * 64
        gb = banks.acquire(4 + lh)
        lhsK = apx(s['RKVF'], base, 64, (lh // 2) * 512 + b * 64, [[1, 64]])
        rhsq = chx(s['RQF'], lh // 2, base, 64)
        k.te(lambda e, gb=gb, lhsK=lhsK, rhsq=rhsq: e.matmul(
            apx(gb, 0, 64, 0, [[1, 512]]), lhsK, rhsq, start=True, stop=True))
        k.te(lambda e, gb=gb, lhsK=lhsK, rhsq=rhsq, tp=(base, 64): e.matmul(
            apx(gb, 64, 64, 0, [[1, 512]]), lhsK, rhsq, start=True, stop=True,
            tile_position=tp))
        pe = k.sig('tensor', 'pe')
        k.wait('scalar', 'pe', pe)
        k.act(lambda e, gb=gb, lh=lh: e.activation(
            chx(s['PLAT'], lh), apx(gb, 0, 128, 0, [[1, 512]]), AF.Exp,
            bias=0.0, scale=0.125))
        banks.release(4 + lh, 'scalar')
    ac = k.sig('scalar', 'ac')
    k.wait('vector', 'ac', ac)
    for lh in range(4):
        k.vec(lambda e, lh=lh: e.tensor_tensor(
            chx(s['PLAT'], lh), chx(s['PLAT'], lh), apx(s['MKLAT']),
            Alu.mult))
    dv = k.sig('vector', 'dv')
    k.wait('tensor', 'dv', dv)
    for qc in range(4):
        vb = banks.acquire(qc % 4)
        for lh in range(4):
            base = (b % 2) * 64
            blk = (b // 2) * 4 + lh
            lhsP = apx(s['PLAT'], base, 64, lh * 512 + qc * 128, [[1, 128]])
            rhsV = apx(s['RVP'], base, 64, blk * 65, [[1, 65]])
            outp = apx(vb, 0, 128, lh * 65, [[1, 65]])
            k.te(lambda e, outp=outp, lhsP=lhsP, rhsV=rhsV, f=(lh == 0), l=(lh == 3):
                 e.matmul(outp, lhsP, rhsV, start=f, stop=l))
        pe = k.sig('tensor', 'pe')
        k.wait('vector', 'pe', pe)
        k.vec(lambda e, vb=vb: e.tensor_scalar(
            apx(s['DEN0']), apx(vb, 0, 128, 64, [[65, 4]]), 1e-12, 0.0,
            Alu.add, Alu.add))
        k.vec(lambda e: e.reciprocal(apx(s['DEN0']), apx(s['DEN0'])))
        dst = chx(s['LAT'], qc)
        k.vec(lambda e, vb=vb, dst=dst: e.tensor_tensor(
            dst, apx(vb, 0, 128, 0, [[65, 4], [1, 64]]),
            apx(s['DEN0'], 0, 128, 0, [[1, 4], [0, 64]]), Alu.mult))
        banks.release(qc % 4, 'vector')
    dv = k.sig('vector', 'dv')
    k.wait('tensor', 'dv', dv)
    for qc in range(4):
        bkt = 4 + qc % 4
        pb = banks.acquire(bkt)
        for cc in range(2):
            src = apx(s['LAT'], 0, 128, qc * 256 + cc * 128, [[1, 128]])
            k.te(lambda e, pb=pb, src=src, cc=cc, f=(cc == 0), l=(cc == 1): e.matmul(
                apx(pb, 0, 128, cc * 128, [[1, 128]]), src, IDENAP,
                start=f, stop=l, is_transpose=True))
        pe = k.sig('tensor', 'pe')
        k.wait('vector', 'pe', pe)
        dst = apx(s['LATF'], 0, 128, qc * 128, [[512, 2], [1, 128]])
        k.vec(lambda e, pb=pb, dst=dst: e.tensor_copy(dst, apx(pb, 0, 128, 0, [[1, 256]])))
        banks.release(bkt, 'vector')
    dv = k.sig('vector', 'dv')
    k.wait('tensor', 'dv', dv)
    for qc in range(4):
        b0 = (qc % 2) * 2
        p1 = banks.acquire(b0)
        p2_ = banks.acquire(b0 + 1)
        for kc in range(2):
            lhs = apx(s['LATF'], 0, 128, kc * 512 + qc * 128, [[1, 128]])
            for ns in range(2):
                pb = (p1, p2_)[ns]
                rhs = chx(s['ROUTW'], kc, off=ns * 512, n=512)
                k.te(lambda e, pb=pb, lhs=lhs, rhs=rhs, f=(kc == 0), l=(kc == 1):
                     e.matmul(apx(pb), lhs, rhs, start=f, stop=l))
        pe = k.sig('tensor', 'pe')
        k.wait('vector', 'pe', pe)
        for ns in range(2):
            pb = (p1, p2_)[ns]
            rbt = apx(s['RBT'], 0, 128, ns * 512, [[1, 512]])
            k.vec(lambda e, pb=pb, qc=qc, ns=ns, rbt=rbt: e.tensor_tensor(
                apx(s['REM'], 0, 128, qc * 1024 + ns * 512, [[1, 512]]), apx(pb),
                rbt, Alu.add))
            banks.release(b0 + ns, 'vector')
    dv = k.sig('vector', 'dv')
    k.wait('scalar', 'dv', dv)
    for qc in range(4):
        k.vec(lambda e, qc=qc: e.tensor_tensor(
            apx(s['SQ0']), chx(s['LOC'], qc), apx(s['GLREP']), Alu.mult))
        k.vec(lambda e, qc=qc: e.reduce_sum(
            apx(s['GATE'], 0, 128, 0, [[2, 16]]),
            apx(s['SQ0'], 0, 128, 0, [[64, 16], [1, 64]]), AX.X))
        k.vec(lambda e, qc=qc: e.tensor_tensor(
            apx(s['SQ0']), chx(s['REM'], qc), apx(s['GRREP']), Alu.mult))
        k.vec(lambda e, qc=qc: e.reduce_sum(
            apx(s['GATE'], 0, 128, 1, [[2, 16]]),
            apx(s['SQ0'], 0, 128, 0, [[64, 16], [1, 64]]), AX.X))
        k.vec(lambda e: e.tensor_tensor(
            apx(s['GATE'], 0, 128, 0, [[2, 16]]), apx(s['GATE'], 0, 128, 0, [[2, 16]]),
            apx(s['GATE'], 0, 128, 1, [[2, 16]]), Alu.add))
        dv = k.sig('vector', 'dv')
        k.wait('scalar', 'dv', dv)
        k.act(lambda e: e.activation(apx(s['GATE'], 0, 128, 0, [[2, 16]]),
                                     apx(s['GATE'], 0, 128, 0, [[2, 16]]),
                                     AF.Sigmoid, bias=0.0, scale=1.0))
        ac = k.sig('scalar', 'ac')
        k.wait('vector', 'ac', ac)
        k.vec(lambda e, qc=qc: e.tensor_tensor(
            chx(s['LOC'], qc), chx(s['LOC'], qc), chx(s['REM'], qc), Alu.subtract))
        k.vec(lambda e, qc=qc: e.tensor_tensor(
            chx(s['LOC'], qc), chx(s['LOC'], qc),
            apx(s['GATE'], 0, 128, 0, [[2, 16], [0, 64]]), Alu.mult))
        k.vec(lambda e, qc=qc: e.tensor_tensor(
            chx(s['LOC'], qc), chx(s['LOC'], qc), chx(s['REM'], qc), Alu.add))
    dv = k.sig('vector', 'dv')
    k.wait('tensor', 'dv', dv)
    for j in range(4):
        transpose_to_fm_split(k, banks, [(2 * j) % 8, (2 * j + 1) % 8], s['LOC'],
                              j * 1024, s['MIXFH'], s['MIXFL'], 0, j, IDENAP)
    k.wait('tensor', 'dv')
    obanks = [banks.acquire(i) for i in range(8)]
    for kc in range(8):
        sl = kc % 2
        ring0.load(sl, dram_ap(dd['ow_h'], kc * 128 * D, [[1, 128 * D]]),
                   off=0, n=1024)
        ring0.load2(sl, chx(s['RING0'], sl, off=1024, n=1024),
                    dram_ap(dd['ow_l'], kc * 128 * D, [[1, 128 * D]]))
        ring0.use(sl)
        for qc in range(4):
            lh = chx(s['MIXFH'], kc, off=qc * 128, n=128)
            ll = chx(s['MIXFL'], kc, off=qc * 128, n=128)
            for ns in range(2):
                pb = obanks[qc * 2 + ns]
                wh = chx(s['RING0'], sl, off=ns * 512, n=512)
                wl = chx(s['RING0'], sl, off=1024 + ns * 512, n=512)
                mm3(k, apx(pb), lh, ll, wh, wl, first=(kc == 0), last=(kc == 7))
        ring0.done(sl)
    pe = k.sig('tensor', 'pe')
    k.wait('vector', 'pe', pe)
    for qc in range(4):
        for ns in range(2):
            k.vec(lambda e, qc=qc, ns=ns: e.tensor_copy(
                apx(s['REM'], 0, 128, qc * 1024 + ns * 512, [[1, 512]]),
                apx(obanks[qc * 2 + ns], 0, 128, 0, [[1, 512]])))
            banks.release(qc * 2 + ns, 'vector')
    for qc in range(4):
        p3 = banks.acquire(2)
        p4 = banks.acquire(3)
        for kc in range(8):
            lh = chx(s['XFH'], kc, off=128 + qc * 128, n=128)
            ll = chx(s['XFL'], kc, off=128 + qc * 128, n=128)
            for ns in range(2):
                pb = (p3, p4)[ns]
                w = chx(s['TERN'], kc, off=ns * 512, n=512)
                k.te(lambda e, pb=pb, lh=lh, w=w, f=(kc == 0): e.matmul(
                    apx(pb), lh, w, start=f, stop=False))
                k.te(lambda e, pb=pb, ll=ll, w=w, l=(kc == 7): e.matmul(
                    apx(pb), ll, w, start=False, stop=l))
        pe = k.sig('tensor', 'pe')
        k.wait('scalar', 'dv')
        xe = k.dma('scalar', 'ye', apx(s['XEP']),
                   dram_ap(dd['xown'], (b * 640 + 128 + qc * 128) * D, [[1, 128 * D]]))
        k.wait('vector', 'ye', xe)
        k.wait('vector', 'pe', pe)
        k.wait('vector', 'yo')
        k.wait('vector', 'yp')
        for ns in range(2):
            pf = (p3, p4)[ns]
            sl512 = apx(s['T1E'], 0, 128, ns * 512, [[1, 512]])
            cbt = apx(s['CBT'], 0, 128, ns * 512, [[1, 512]])
            k.vec(lambda e, pf=pf, sl512=sl512, cbt=cbt: e.scalar_tensor_tensor(
                sl512, apx(pf), apx(s['ALF'], 0, 128, 1, [[1, 1]]), cbt,
                Alu.mult, Alu.add))
            k.vec(lambda e, qc=qc, ns=ns, sl512=sl512: e.tensor_tensor(
                sl512, sl512, apx(s['REM'], 0, 128, qc * 1024 + ns * 512, [[1, 512]]),
                Alu.add))
            k.vec(lambda e, ns=ns, sl512=sl512: e.tensor_tensor(
                sl512, sl512, apx(s['XEP'], 0, 128, ns * 512, [[1, 512]]), Alu.add))
        banks.release(2, 'vector')
        banks.release(3, 'vector')
        dv = k.sig('vector', 'dv')
        qq = 'yo' if qc % 2 else 'yp'
        qe = 'scalar' if qc % 2 else 'sync'
        k.wait(qe, 'dv', dv)
        k.dma(qe, qq, dram_ap(dd['y0'], (qc * 128 * 8 + b) * D,
                              [[8 * D, 128], [1, D]]), apx(s['T1E']))


def p0_run(B, nb=NB):
    k, s = B.k, B.s
    ring0 = Ring(k, s['RING0'], q2='scalar')
    xl = k.dma('scalar', 'xl', apx(s['ROUTW']),
               dram_ap(B.d['routw'], 0, [[D, 128], [128 * D, 2], [1, D]]))
    k.wait('tensor', 'xl', xl)
    for b in range(nb):
        p0_batch(B, b, ring0)
    k.wait('gpsimd', 'yo')
    k.wait('gpsimd', 'yp')

from concourse.bass_utils import run_bass_kernel_spmd

_CACHED = {}


def build_full(skip_p0=False, skip_p2=False):
    import os
    skip_p0 = skip_p0 or bool(os.environ.get('K_SKIP_P0'))
    skip_p2 = skip_p2 or bool(os.environ.get('K_SKIP_P2'))
    if 'nc' in _CACHED:
        return _CACHED['nc']
    B = Builder(unit='full', n_tiles=8)
    k, sb, s = B.k, B.sb, B.s
    p0_declare(B)
    p2_declare(B)
    p0_alloc_consts(B)
    s['CONST'] = sb.t4('CONST', [128, 4])
    k.vec(lambda e: e.memset(apx(s['CONST'], 0, 128, 0, [[1, 1]]), EPS))
    B.nc.const_aps.register(F32, EPS, apx(s['CONST'], 0, 128, 0, [[1, 1]]))
    p0_load_consts(B)
    p2_load_consts(B)
    sb.mark()
    p0_alloc(B)
    k.wait('vector', 'xl')
    k.wait('tensor', 'xl')
    k.wait('scalar', 'xl')
    p0_prep_weights(B)
    p2_prep_qkv(B)
    p0_bias_init(B)
    stage = os.environ.get('K_P0_STAGE', 'all')
    if not skip_p0:
        if stage in ('pha', 'rkv', 'all') or stage.startswith('b'):
            p0_pha(B)
            B.k.barrier()
        if stage in ('rkv', 'all') or stage.startswith('b'):
            p0_rkv(B, 0)
            B.k.barrier()
        p2_prep_tern(B)
        if stage == 'all' or stage.startswith('b'):
            p0_run(B, nb=(NB if stage == 'all' else int(stage[1:])))
            B.k.barrier()
    else:
        p2_prep_tern(B)
        B.k.barrier()
    sb.release()
    sb.mark()
    p2_alloc(B)
    p2_init(B)
    if not skip_p2:
        p2_run(B)
    B.k.barrier()
    sb.release()
    _CACHED['nc'] = B.nc
    _CACHED['B'] = B
    return B.nc


def _consts_per_core(core):
    d = {}
    d['iden'] = np.eye(128, dtype=np.float32)
    r = np.arange(128)[:, None]
    c = np.arange(128)[None, :]
    mA = np.where(r >= c, 0.0, NEG).astype(np.float32)
    mB = np.where(r < c, 0.0, NEG).astype(np.float32)
    mk = np.ascontiguousarray(np.concatenate([mA, mB], axis=1))
    mk0 = (np.ascontiguousarray(
        np.concatenate([np.full((128, 128), NEG, np.float32), mB], axis=1))
        if core == 0 else mk)
    d['mkab'] = np.ascontiguousarray(np.tile((mk == 0).astype(np.float32), (1, 2)))
    d['mkab0'] = np.ascontiguousarray(np.tile((mk0 == 0).astype(np.float32), (1, 2)))
    lat = np.arange(64)[:, None]
    qq = np.arange(512)[None, :]
    mkl = (lat < (8 * core + qq // 64)).astype(np.float32)
    d['mklat'] = np.ascontiguousarray(np.concatenate([mkl, mkl], axis=0))

    m = np.full((128, 128), NEG, np.float32)
    for s1 in range(16):
        for b1 in range(8):
            for b in range(8):
                if abs(b1 - b) <= 5:
                    m[s1 * 8 + b1, s1 * 8 + b] = 0.0
    d['maskl1'] = m
    o = np.zeros((128, 512), np.float32)
    for hs in range(4):
        for w in range(11):
            for b in range(8):
                j = b + w - 5
                if j < 0 or j >= 8:
                    o[32 * hs + w, b::8] = 1.0
    d['oobm'] = o
    sel = np.zeros((128, 4), np.float32)
    for n in range(4):
        sel[32 * n:32 * n + 11, n] = 1.0
    d['sel4'] = sel
    dl = np.zeros((128, 128), np.float32)
    for sl in range(4):
        for s1 in range(16):
            for b1 in range(8):
                dl[32 * sl + b1, s1 * 8 + b1] = 1.0
    d['delta4'] = dl
    return d


def stage_inputs(inputs):
    x = np.ascontiguousarray(np.asarray(inputs['L0_attn_x'], dtype=np.float32))
    shared = {}
    shared['xfull'] = np.ascontiguousarray(x.reshape(32768, D))
    shared['qkvw'] = np.ascontiguousarray(np.asarray(inputs['L0_attn_qkv_w']))
    shared['qkvbf'] = np.ascontiguousarray(
        np.asarray(inputs['L0_attn_qkv_b']).reshape(24, 128).T)
    shared['rqw'] = np.ascontiguousarray(np.asarray(inputs['L0_attn_rq_w']))
    shared['rqbf'] = np.ascontiguousarray(
        np.asarray(inputs['L0_attn_rq_b']).reshape(2, 128).T)
    shared['rkvw'] = np.ascontiguousarray(np.asarray(inputs['L0_attn_rkv_w']))
    shared['rkvbf'] = np.ascontiguousarray(
        np.asarray(inputs['L0_attn_rkv_b']).reshape(4, 128).T)
    shared['routw'] = np.ascontiguousarray(np.asarray(inputs['L0_attn_rout_w']))

    shared['outw'] = np.ascontiguousarray(np.asarray(inputs['L0_attn_out_w']))

    shared['w0t'] = np.ascontiguousarray(np.asarray(inputs['L0_ffn_weight']).T)
    shared['alpha1'] = np.asarray(inputs['L0_ffn_alpha']).reshape(1, 1)
    shared['routbr'] = np.asarray(inputs['L0_attn_rout_b']).reshape(1, D)
    shared['outbr'] = np.asarray(inputs['L0_attn_out_b']).reshape(1, D)
    shared['fbiasr'] = np.asarray(inputs['L0_ffn_bias']).reshape(1, D)
    shared['vbr'] = np.ascontiguousarray(
        np.asarray(inputs['L0_attn_qkv_b'])[2048:].reshape(1, D))
    gl = np.asarray(inputs['L0_attn_gl_w']).ravel()
    gr = np.asarray(inputs['L0_attn_gr_w']).ravel()
    shared['glrep'] = np.ascontiguousarray(
        np.tile(np.tile(gl, 16)[None, :], (128, 1)).astype(np.float32))
    shared['grrep'] = np.ascontiguousarray(
        np.tile(np.tile(gr, 16)[None, :], (128, 1)).astype(np.float32))

    shared['wqt'] = np.ascontiguousarray(np.asarray(inputs['L1_attn_Wq']).T)
    shared['wkt'] = np.ascontiguousarray(np.asarray(inputs['L1_attn_Wk']).T)
    shared['wvt'] = np.ascontiguousarray(np.asarray(inputs['L1_attn_Wv']).T)
    pe = np.asarray(inputs['L1_attn_pe'])
    pev = np.zeros((64, 16, 8, 8), np.float32)
    for b in range(8):
        for j in range(8):
            w = j - b + 5
            if 0 <= w < 11:
                pev[:, :, b, j] = pe[:, :, w].T
    shared['pev'] = np.ascontiguousarray(
        np.concatenate([pev.reshape(64, -1)] * 2, axis=0))
    pew = pe.transpose(1, 0, 2).reshape(64, 16 * 11)
    shared['pew'] = np.ascontiguousarray(np.concatenate([pew] * 2, axis=0))
    shared['w1r'] = np.ascontiguousarray(
        np.asarray(inputs['L1_ffn_w1_w']).T.reshape(8, 128, 32, 128).transpose(2, 1, 0, 3))
    shared['w2r'] = np.ascontiguousarray(
        np.asarray(inputs['L1_ffn_w2_w']).T.reshape(8, 128, 32, 128).transpose(2, 1, 0, 3))
    shared['h1r'] = np.ascontiguousarray(
        np.asarray(inputs['L1_ffn_h1_w']).T.reshape(32, 128, 32, 128).transpose(2, 1, 0, 3))
    shared['h2r'] = np.ascontiguousarray(
        np.asarray(inputs['L1_ffn_h2_w']).T.reshape(32, 128, 32, 128).transpose(2, 1, 0, 3))
    shared['w3r'] = np.ascontiguousarray(
        np.asarray(inputs['L1_ffn_w3_w']).T.reshape(32, 128, 8, 128).transpose(2, 1, 0, 3))
    shared['s1f'] = np.ascontiguousarray(np.asarray(inputs['L1_ffn_w1_s']).reshape(32, 128).T)
    shared['s2f'] = np.ascontiguousarray(np.asarray(inputs['L1_ffn_w2_s']).reshape(32, 128).T)
    shared['hs1f'] = np.ascontiguousarray(np.asarray(inputs['L1_ffn_h1_s']).reshape(32, 128).T)
    shared['hs2f'] = np.ascontiguousarray(np.asarray(inputs['L1_ffn_h2_s']).reshape(32, 128).T)
    shared['s3f'] = np.ascontiguousarray(np.asarray(inputs['L1_ffn_w3_s']).reshape(8, 128).T)
    maps = []
    for core in range(8):
        d = dict(shared)
        d.update(_consts_per_core(core))
        xo = np.zeros((8, 640, D), np.float32)
        s0 = core * 512
        for b in range(NB):
            if s0 >= 128:
                xo[b, :128] = x[b, s0 - 128:s0]
            xo[b, 128:] = x[b, s0:s0 + 512]
        d['xown'] = xo
        maps.append(d)
    return maps


def kernel(**inputs):
    nc = build_full()
    maps = stage_inputs(inputs)
    res = run_bass_kernel_spmd(nc, maps, core_ids=list(range(8)))
    out = np.empty((NB, 4096, D), dtype=np.float32)
    for i in range(8):
        yo = np.asarray(res.results[i]['yout']).reshape(512, 8, D)
        out[:, i * 512:(i + 1) * 512, :] = yo.transpose(1, 0, 2)
    return out
